# version 09 (final)
# baseline (speedup 1.0000x reference)
import numpy as np
import ml_dtypes
import concourse.bass as bass
import concourse.mybir as mybir
from concourse.bass_utils import run_bass_kernel_spmd

F32 = mybir.dt.float32
BF16 = mybir.dt.bfloat16
AF = mybir.ActivationFunctionType
ALU = mybir.AluOpType
AX = mybir.AxisListType
BF16NP = ml_dtypes.bfloat16

D = 2048
S = 2048
B = 4
H = 16
NOPE = 128
ROPE = 64
QH = NOPE + ROPE
LORA = 512
VD = 128
NSLOT = 8
TPC = NSLOT * 128
NEXP = 16
FI = 1408
FS = 2816
SCALE = float(QH) ** -0.5
EPS_LN = 1e-5
EPS_RMS = 1e-6
NEG = -3.0e38

STAGE = 1
FORCE_V_EVAC = False
DBG_DELAY = False
DBG_HM = None
W3_ON_SYNC = True
IT_BARRIER = False
IT_LIST = None
ENGS = ("tensor", "vector", "scalar", "gpsimd", "sync")


class Sem:
    def __init__(self, h):
        self.h = h
        self.n = 0

    def tick(self, k=1):
        self.n += k
        return self.n


class Emit:

    def __init__(self, block):
        self.block = block
        self.bufs = {e: [] for e in ENGS}
        self.lastw = {e: {} for e in ENGS}

    def _dedupe(self, eng, waits):
        m = {}
        for (s, v) in waits:
            if s is None:
                continue
            k = id(s)
            if k not in m or v > m[k][1]:
                m[k] = (s, int(v))
        lw = self.lastw[eng]
        wl = []
        for k, (s, v) in m.items():
            if lw.get(k, -1) >= v:
                continue
            lw[k] = v
            wl.append((s.h, v))
        return wl

    def op(self, eng, build, waits=(), tick=None, tickk=1):
        wl = self._dedupe(eng, waits)
        wp = tick.tick(tickk) if tick is not None else None
        th = tick.h if tick is not None else None

        def f(h, wl=wl, th=th, tickk=tickk, build=build):
            for (sh, v) in wl:
                h.wait_ge(sh, v)
            ins = build(h)
            if th is not None:
                ins.then_inc(th, tickk)

        self.bufs[eng].append(f)
        return wp

    def wait(self, eng, waits):
        wl = self._dedupe(eng, waits)

        def f(h, wl=wl):
            for (sh, v) in wl:
                h.wait_ge(sh, v)

        self.bufs[eng].append(f)

    def barrier(self, sems, engines=ENGS):
        vals = [(s, s.n) for s in sems]
        for e in engines:
            self.wait(e, vals)

    def flush(self):
        for eng in ENGS:
            fns = self.bufs[eng]
            if not fns:
                continue
            self.bufs[eng] = []
            dec = getattr(self.block, eng)

            def body(h, fns=fns):
                for f in fns:
                    f(h)

            dec(body)


def r3(t, k):
    return t.rearrange("p (k m) -> p k m", k=k)


class Arena:

    def __init__(self, nc, kb):
        self.base = nc.alloc_sbuf_tensor("arena", [128, kb * 1024], mybir.dt.uint8).ap()
        self.cursor = 0
        self.hi = 0

    def at(self, off, P, F, dtype):
        nb = F * mybir.dt.size(dtype)
        ap = self.base[:, off:off + nb].bitcast(dtype)
        if P < 128:
            ap = ap[0:P, :]
        self.hi = max(self.hi, off + nb)
        return ap

    def seq(self, P, F, dtype, align=64):
        off = (self.cursor + align - 1) // align * align
        ap = self.at(off, P, F, dtype)
        self.cursor = off + F * mybir.dt.size(dtype)
        return ap


def build_program(stage=STAGE, n_p1=NSLOT, ln_parts=4, do_kvs=True, do_w=True, n_groups=8, n_slots=NSLOT, n_iters=18, dbg_moe=False):
    nc = bass.Bass("TRN2", target_bir_lowering=False, debug=False)

    xm_d = nc.dram_tensor("xm", [TPC, D], F32, kind="ExternalInput").ap()
    xb_d = nc.dram_tensor("xb", [S, D], F32, kind="ExternalInput").ap()
    wq_d = nc.dram_tensor("wq", [D, H * QH], F32, kind="ExternalInput").ap()
    wkva_d = nc.dram_tensor("wkva", [D, LORA + ROPE], F32, kind="ExternalInput").ap()
    wkvb_d = nc.dram_tensor("wkvb", [LORA, H * (NOPE + VD)], F32, kind="ExternalInput").ap()
    wo_d = nc.dram_tensor("wo", [H * VD, D], F32, kind="ExternalInput").ap()
    kvs_d = nc.dram_tensor("kvs", [LORA], F32, kind="ExternalInput").ap()
    gw_d = nc.dram_tensor("gw", [NEXP, D], F32, kind="ExternalInput").ap()
    w13_d = nc.dram_tensor("w13", [18, 11, 128, 4096], BF16, kind="ExternalInput").ap()
    w2t_d = nc.dram_tensor("w2t", [18, 128, 11 * 2048], BF16, kind="ExternalInput").ap()
    cq_d = nc.dram_tensor("cq", [ROPE, TPC], F32, kind="ExternalInput").ap()
    sq_d = nc.dram_tensor("sq", [ROPE, TPC], F32, kind="ExternalInput").ap()
    ck_d = nc.dram_tensor("ck", [32, S], F32, kind="ExternalInput").ap()
    sk_d = nc.dram_tensor("sk", [32, S], F32, kind="ExternalInput").ap()
    qmask_d = nc.dram_tensor("qmask", [128, 256], F32, kind="ExternalInput").ap()

    y_d = None
    if stage >= 4:
        y_d = nc.dram_tensor("y", [D, TPC], F32, kind="ExternalOutput").ap()
    taps = {}
    if stage <= 1:
        taps["t_ln1m"] = nc.dram_tensor("t_ln1m", [128, 16 * TPC], BF16, kind="ExternalOutput").ap()
    if stage == 1:
        taps["t_ckv"] = nc.dram_tensor("t_ckv", [128, 4 * S], BF16, kind="ExternalOutput").ap()
        taps["t_kpe"] = nc.dram_tensor("t_kpe", [64, S], BF16, kind="ExternalOutput").ap()
    if stage == 2:
        taps["t_ofm"] = nc.dram_tensor("t_ofm", [128, 16 * TPC], BF16, kind="ExternalOutput").ap()
    if dbg_moe:
        taps["t_h"] = nc.dram_tensor("t_h", [128, 11 * 1024], BF16, kind="ExternalOutput").ap()
        taps["t_w13"] = nc.dram_tensor("t_w13", [128, 2 * 2 * 16 * 128], BF16, kind="ExternalOutput").ap()
        taps["t_w2a"] = nc.dram_tensor("t_w2a", [128, 6 * 2048], BF16, kind="ExternalOutput").ap()
        taps["t_ln2b"] = nc.dram_tensor("t_ln2b", [128, 16 * TPC], BF16, kind="ExternalOutput").ap()
    if stage == 3:
        taps["t_yacc"] = nc.dram_tensor("t_yacc", [128, 16 * TPC], F32, kind="ExternalOutput").ap()
        taps["t_ln2"] = nc.dram_tensor("t_ln2", [128, 16 * TPC], BF16, kind="ExternalOutput").ap()
        taps["t_dw"] = nc.dram_tensor("t_dw", [128, 128], F32, kind="ExternalOutput").ap()
        taps["t_dwT"] = nc.dram_tensor("t_dwT", [16, 1024], F32, kind="ExternalOutput").ap()
        taps["t_qn"] = nc.dram_tensor("t_qn", [128, 2 * 1024], BF16, kind="ExternalOutput").ap()
        taps["t_qpe"] = nc.dram_tensor("t_qpe", [64, 2 * 1024], BF16, kind="ExternalOutput").ap()
        taps["t_kn"] = nc.dram_tensor("t_kn", [128, 2 * 2048], BF16, kind="ExternalOutput").ap()
        taps["t_v"] = nc.dram_tensor("t_v", [128, 16 * 256], BF16, kind="ExternalOutput").ap()
        taps["t_srow"] = nc.dram_tensor("t_srow", [128, S], F32, kind="ExternalOutput").ap()
        taps["t_abf"] = nc.dram_tensor("t_abf", [128, S], BF16, kind="ExternalOutput").ap()

    id_bf_d = nc.inline_tensor(np.eye(128, dtype=BF16NP), "id_bf")
    id_f_d = nc.inline_tensor(np.eye(128, dtype=np.float32), "id_f")
    ones_col_d = nc.inline_tensor(np.ones((128, 1), dtype=BF16NP), "ones_col")
    ones_row_d = nc.inline_tensor(np.ones((1, 128), dtype=np.float32), "ones_row")
    oh_np = np.zeros((16, 16 * 128), dtype=BF16NP)
    for e in range(16):
        oh_np[e, e * 128:(e + 1) * 128] = 1
    oh_d = nc.inline_tensor(oh_np, "onehot")
    epsc_d = nc.inline_tensor(
        np.tile(np.array([[EPS_LN, EPS_RMS]], dtype=np.float32), (128, 1)), "epsc")

    from contextlib import ExitStack
    es = ExitStack()
    semn = ["sQ", "sG", "sC", "sP", "sV", "sA", "sO", "sPL", "sX0", "sX1",
            "sWQ", "sWO0", "sWO1", "sGW", "sW20", "sB0", "sB1", "sB2"]
    SH = {n: es.enter_context(nc.semaphore(n)) for n in semn}
    pb0 = es.enter_context(nc.psum_tensor("pb0", [128, 512], F32))
    pb1 = es.enter_context(nc.psum_tensor("pb1", [128, 512], F32))
    pb2 = es.enter_context(nc.psum_tensor("pb2", [128, 512], F32))
    pb3 = es.enter_context(nc.psum_tensor("pb3", [128, 512], F32))
    pb4 = es.enter_context(nc.psum_tensor("pb4", [128, 512], F32))
    pb5 = es.enter_context(nc.psum_tensor("pb5", [128, 512], F32))
    tpA = es.enter_context(nc.psum_tensor("tpA", [128, 1024], BF16))
    tpB = es.enter_context(nc.psum_tensor("tpB", [128, 1024], BF16))
    with nc.Block() as block:
        sQ, sG, sC, sP, sV, sA, sO = (Sem(SH[n]) for n in ("sQ", "sG", "sC", "sP", "sV", "sA", "sO"))
        sPL, sX0, sX1, sWQ, sWO0, sWO1, sGW, sW2 = (
            Sem(SH[n]) for n in ("sPL", "sX0", "sX1", "sWQ", "sWO0", "sWO1", "sGW", "sW20"))
        sW13 = [Sem(SH[n]) for n in ("sB0", "sB1", "sB2")]
        sX = [sX0, sX1]
        sWO = [sWO0, sWO1]
        ALL_SEMS = [sQ, sG, sC, sP, sV, sA, sPL, sX0, sX1, sWQ, sWO0, sWO1, sGW, sW2] + sW13
        em = Emit(block)
        pb = [pb0, pb1, pb2, pb3, pb4, pb5]
        tp = [tpA, tpB]

        ar = Arena(nc, 207)
        id_bf = ar.seq(128, 128, BF16)
        id_f = ar.seq(128, 128, F32)
        ones_col = ar.seq(128, 1, BF16)
        ones_row = ar.seq(1, 128, F32)
        kvsc = ar.seq(128, 4, F32)
        cols = ar.seq(128, 16, F32)
        st24 = ar.seq(128, 24, F32)
        ag2 = ar.seq(128, 2, F32)
        epsc = ar.seq(128, 2, F32)
        oh_sb = ar.seq(16, 16 * 128, BF16)
        att_base = ar.cursor
        ln1m = ar.seq(128, 16 * TPC, BF16)
        ckv = ar.seq(128, 4 * S, BF16)
        kpe = ar.seq(64, S, BF16)
        wkvb = ar.seq(128, 4 * 4096, BF16)
        p12_base = ar.cursor
        xt = [ar.seq(128, D, F32) for i in range(2)]
        lnbf = [ar.seq(128, D, BF16) for i in range(2)]
        lnst = ar.seq(128, 16 * 512, BF16)
        wkva = ar.seq(128, 16 * 576, BF16)
        ckvf = ar.seq(128, 4 * 512, F32)
        sqbf = ar.seq(128, 4 * 512, BF16)
        ckt = ar.seq(32, S, F32)
        skt = ar.seq(32, S, F32)
        kxw = ar.seq(32, 6 * 512, F32)
        rr_row = ar.seq(1, 2 * 512, F32)
        kvs4 = ar.seq(4, 128, F32)

        for dst, src in ((id_bf, id_bf_d), (id_f, id_f_d), (ones_col, ones_col_d),
                         (ones_row, ones_row_d), (epsc, epsc_d), (oh_sb, oh_d)):
            em.op("sync", lambda h, dst=dst, src=src: h.dma_start(out=dst[:, :], in_=src.ap()),
                  tick=sQ, tickk=16)
        em.op("sync", lambda h: h.dma_start(out=ckt[:, :], in_=ck_d), tick=sQ, tickk=16)
        em.op("sync", lambda h: h.dma_start(out=skt[:, :], in_=sk_d), tick=sQ, tickk=16)
        wp_consts = sQ.n

        wp_kvse = None
        if do_kvs:
            wp_kvsl = em.op("gpsimd", lambda h: h.dma_start(
                out=kvs4[:, :], in_=kvs_d.rearrange("(k p) -> k p", p=128)), tick=sPL, tickk=16)
            wp_kvst = em.op("tensor", lambda h: h.transpose(
                out=pb5[:, 0:4], in_=kvs4[:, :], identity=id_f[0:4, 0:4]),
                waits=[(sPL, 48), (sQ, wp_consts)], tick=sP)
            wp_kvse = em.op("vector", lambda h: h.tensor_copy(out=kvsc[:, :], in_=pb5[:, 0:4]),
                            waits=[(sP, wp_kvst)], tick=sV)

        wp_wkva = wp_wkvb = None
        if do_w:
            wp_wkva = em.op("gpsimd", lambda h: h.dma_start(
                out=r3(wkva, 16), in_=wkva_d.rearrange("(k p) o -> p k o", p=128)), tick=sPL, tickk=16)
            wp_wkvb = em.op("gpsimd", lambda h: h.dma_start(
                out=r3(wkvb, 4), in_=wkvb_d.rearrange("(k p) o -> p k o", p=128)), tick=sPL, tickk=16)
            wp_wkva = wp_wkvb = 48

        tpar = [0]
        tp_guard = [None, None]

        def ln_chunk(par, x_wait, dest, dest_k, dest_col, evac_extra_waits=()):
            def bns(h, par=par):
                h.bn_stats(out=st24[:, 0:6], in_=xt[par][:, 0:512])
                h.bn_stats(out=st24[:, 6:12], in_=xt[par][:, 512:1024])
                h.bn_stats(out=st24[:, 12:18], in_=xt[par][:, 1024:1536])
                return h.bn_stats(out=st24[:, 18:24], in_=xt[par][:, 1536:2048])
            wp1 = em.op("vector", bns, waits=[x_wait], tick=sV)
            wp2 = em.op("vector", lambda h: h.bn_aggr(out=ag2[:, :], in_=st24[:, :]),
                        waits=[(sV, wp1)], tick=sV)
            wp3 = em.op("scalar", lambda h: h.activation(
                out=cols[:, 0:1], in_=ag2[:, 1:2], func=AF.Sqrt, bias=epsc[:, 0:1]),
                waits=[(sV, wp2)], tick=sA)
            wp4 = em.op("vector", lambda h: h.reciprocal(out=cols[:, 1:2], in_=cols[:, 0:1]),
                        waits=[(sA, wp3)], tick=sV)
            wp5 = em.op("vector", lambda h: h.tensor_mul(out=cols[:, 2:3], in0=ag2[:, 0:1],
                                                         in1=cols[:, 1:2]),
                        waits=[(sV, wp4)], tick=sV)
            wp6 = em.op("vector", lambda h, par=par: h.tensor_scalar(
                out=lnbf[par][:, :], in0=xt[par][:, :],
                scalar1=cols[:, 1:2], scalar2=cols[:, 2:3],
                op0=ALU.mult, op1=ALU.subtract),
                waits=[(sV, wp5)], tick=sV)
            lastV = None
            lastA = None
            dv = r3(dest, dest_k)
            for g in range(2):
                half = tpar[0] % 2
                tpar[0] += 1
                twaits = [(sV, wp6), (sQ, wp_consts)]
                if tp_guard[half] is not None:
                    twaits.append(tp_guard[half])

                def tgrp(h, g=g, half=half, par=par):
                    ins = None
                    for j in range(8):
                        dc = 8 * g + j
                        ins = h.transpose(
                            out=tp[half][:, j * 128:(j + 1) * 128],
                            in_=lnbf[par][:, dc * 128:(dc + 1) * 128],
                            identity=id_bf[:, :])
                    return ins
                wpt = em.op("tensor", tgrp, waits=twaits, tick=sP)
                eng, sem = (("vector", sV) if (g % 2 == 0 or FORCE_V_EVAC) else ("scalar", sA))

                def evac(h, g=g, half=half, dv=dv, dest_col=dest_col, eng=eng):
                    src = r3(tp[half], 8)
                    dst = dv[:, 8 * g:8 * g + 8, dest_col * 128:(dest_col + 1) * 128]
                    if eng == "vector":
                        return h.tensor_copy(out=dst, in_=src)
                    return h.copy(out=dst, in_=src)
                ewaits = [(sP, wpt)] + list(evac_extra_waits)
                wpe = em.op(eng, evac, waits=ewaits, tick=sem)
                tp_guard[half] = (sem, wpe)
                if eng == "vector":
                    lastV = wpe
                else:
                    lastA = wpe
            done = [(sem_, wp_) for (sem_, wp_) in ((sV, lastV), (sA, lastA)) if wp_ is not None]
            return done, wp6

        xm_t = xm_d.rearrange("(c p) d -> c p d", p=128)
        xb_t = xb_d.rearrange("(c p) d -> c p d", p=128)
        xt_guard = [None, None]
        for cm in range(n_p1):
            par = cm % 2

            def ldx(h, cm=cm, par=par):
                return h.dma_start(out=xt[par][:, :], in_=xm_t[cm, :, :])
            wp_x = em.op("gpsimd", ldx, waits=(xt_guard[par] or []), tick=sX[par], tickk=16)
            done, _ = ln_chunk(par, (sX[par], wp_x), ln1m, 16, cm)
            xt_guard[par] = done

        wkva_v = r3(wkva, 16)
        ckv_v = r3(ckv, 4)
        SKIP_P2 = (stage == 0)
        kva_last = None
        psum_guard = []
        pb5_guard = None
        pb6_guard = []
        for tt in range(4 if not SKIP_P2 else 0):
            stg_done = []
            for cc in range(4):
                c = tt * 4 + cc
                par = c % 2

                def ldx(h, c=c, par=par):
                    return h.dma_start(out=xt[par][:, :], in_=xb_t[c, :, :])
                wp_x = em.op("gpsimd", ldx, waits=(xt_guard[par] or []), tick=sX[par], tickk=16)
                extra = [( sP, kva_last[1])] if kva_last is not None else []
                done, _ = ln_chunk(par, (sX[par], wp_x), lnst, 16, cc, evac_extra_waits=extra)
                xt_guard[par] = done
                stg_done += done

            kva_wps = []
            for oc in range(5):
                P = 128 if oc < 4 else 64

                def kmm(h, oc=oc, P=P):
                    ins = None
                    for dc in range(16):
                        ins = h.matmul(out=pb[oc][0:P, 0:512],
                                       lhsT=wkva_v[:, dc, oc * 128: oc * 128 + P],
                                       rhs=lnst[:, dc * 512:(dc + 1) * 512],
                                       start=(dc == 0), stop=(dc == 15))
                    return ins
                waits = list(stg_done) + [(sPL, 48)] + psum_guard
                kva_wps.append(em.op("tensor", kmm, waits=waits, tick=sP))
            kva_last = (sP, kva_wps[-1])

            ck_evacA = None
            for oc in range(4):
                def ev1(h, oc=oc):
                    return h.copy(out=ckvf[:, oc * 512:(oc + 1) * 512], in_=pb[oc][:, 0:512])
                em.op("scalar", ev1, waits=[(sP, kva_wps[oc])], tick=sA)

                def ev2(h, oc=oc):
                    return h.activation(out=sqbf[:, oc * 512:(oc + 1) * 512],
                                        in_=pb[oc][:, 0:512], func=AF.Square)
                ck_evacA = em.op("scalar", ev2, waits=[(sP, kva_wps[oc])], tick=sA)
            wp_kx1 = em.op("vector", lambda h: h.tensor_copy(out=kxw[:, 0:512], in_=pb4[0:32, 0:512]),
                           waits=[(sP, kva_wps[4])], tick=sV)
            wp_kx2 = em.op("vector", lambda h: h.tensor_copy(out=kxw[:, 512:1024], in_=pb4[32:64, 0:512]),
                           waits=[(sP, kva_wps[4])], tick=sV)
            psum_guard = [(sA, ck_evacA), (sV, wp_kx2)]

            def rms_mm(h):
                ins = None
                for oc in range(4):
                    ins = h.matmul(out=pb5[0:1, 0:512], lhsT=ones_col[:, :],
                                   rhs=sqbf[:, oc * 512:(oc + 1) * 512],
                                   start=(oc == 0), stop=(oc == 3))
                return ins
            rwaits = [(sA, ck_evacA), (sV, wp_kvse)] + pb6_guard + ([pb5_guard] if pb5_guard else [])
            wp_rms = em.op("tensor", rms_mm, waits=rwaits, tick=sP)
            wp_sq = em.op("scalar", lambda h: h.activation(
                out=rr_row[0:1, 0:512], in_=pb5[0:1, 0:512], func=AF.Sqrt,
                scale=1.0 / float(LORA), bias=epsc[0:1, 1:2]),
                waits=[(sP, wp_rms)], tick=sA)
            pb5_guard = (sA, wp_sq)
            wp_rc = em.op("vector", lambda h: h.reciprocal(out=rr_row[0:1, 512:1024],
                                                           in_=rr_row[0:1, 0:512]),
                          waits=[(sA, wp_sq)], tick=sV)
            wp_bc = em.op("tensor", lambda h: h.matmul(
                out=pb5[:, 0:512], lhsT=ones_row[:, :], rhs=rr_row[0:1, 512:1024],
                start=True, stop=True),
                waits=[(sV, wp_rc)], tick=sP)

            pb6_guard = []
            for oc in range(4):
                def nrm(h, oc=oc, tt=tt):
                    return h.scalar_tensor_tensor(
                        out=ckv_v[:, oc, tt * 512:(tt + 1) * 512],
                        in0=ckvf[:, oc * 512:(oc + 1) * 512],
                        scalar=kvsc[:, oc:oc + 1],
                        in1=pb5[:, 0:512],
                        op0=ALU.mult, op1=ALU.mult)
                wpn = em.op("vector", nrm, waits=[(sP, wp_bc), (sV, wp_kvse)], tick=sV)
                pb6_guard.append((sV, wpn))

            def kprod(h, tt=tt):
                h.tensor_mul(out=kxw[:, 1024:1536], in0=kxw[:, 0:512],
                             in1=ckt[:, tt * 512:(tt + 1) * 512])
                h.tensor_mul(out=kxw[:, 1536:2048], in0=kxw[:, 512:1024],
                             in1=skt[:, tt * 512:(tt + 1) * 512])
                h.tensor_mul(out=kxw[:, 2048:2560], in0=kxw[:, 0:512],
                             in1=skt[:, tt * 512:(tt + 1) * 512])
                return h.tensor_mul(out=kxw[:, 2560:3072], in0=kxw[:, 512:1024],
                                    in1=ckt[:, tt * 512:(tt + 1) * 512])
            wp_kp = em.op("vector", kprod, waits=[(sV, wp_kx2), (sQ, wp_consts)], tick=sV)

            def kfin(h, tt=tt):
                h.tensor_sub(out=kpe[0:32, tt * 512:(tt + 1) * 512],
                             in0=kxw[:, 1024:1536], in1=kxw[:, 1536:2048])
                return h.tensor_add(out=kpe[32:64, tt * 512:(tt + 1) * 512],
                                    in0=kxw[:, 2560:3072], in1=kxw[:, 2048:2560])
            em.op("vector", kfin, waits=[(sV, wp_kp)], tick=sV)


        if stage >= 2:
            em.barrier(ALL_SEMS)
            ar.cursor = p12_base
            wq_g = ar.seq(128, 16 * 384, BF16)
            knope_g = ar.seq(128, 2 * 2048, BF16)
            v_g = ar.seq(128, 16 * 256, BF16)
            qn_g = ar.seq(128, 2 * 1024, BF16)
            qpe_g = ar.seq(64, 2 * 1024, BF16)
            cq64 = ar.seq(64, TPC, F32)
            sq64 = ar.seq(64, TPC, F32)
            qrw = ar.seq(64, 2 * 1024, F32)
            qab = ar.seq(64, 2 * 1024, F32)
            srow = ar.seq(128, S, F32)
            abf_r = ar.seq(128, S, BF16)
            anm_r = abf_r
            atT = ar.seq(128, S, BF16)
            qmk = ar.seq(128, 256, F32)
            o_fm = ar.seq(128, 16 * TPC, BF16)

            wp_cq = em.op("sync", lambda h: h.dma_start(out=cq64[:, :], in_=cq_d), tick=sQ, tickk=16)
            wp_sq_ = em.op("sync", lambda h: h.dma_start(out=sq64[:, :], in_=sq_d), tick=sQ, tickk=16)
            wp_qm = em.op("sync", lambda h: h.dma_start(out=qmk[:, :], in_=qmask_d), tick=sQ, tickk=16)
            wp_tab = sQ.n

            wkvb_4 = wkvb.rearrange("p (k e c) -> p k e c", k=4, e=16)
            ln1m_v = r3(ln1m, 16)
            wqg_v = r3(wq_g, 16)

            bankg = {}
            evac_flip = [0]

            def ev_eng():
                evac_flip[0] ^= 1
                return ("vector", sV) if evac_flip[0] else ("scalar", sA)

            def accum(bank_t, bank_key, mms, evac, mm_waits, ev_waits, eng=None, sem=None):
                w = list(mm_waits)
                if bank_key in bankg:
                    w.append(bankg[bank_key])
                wpm = em.op("tensor", mms, waits=w, tick=sP)
                if eng is None:
                    eng, sem = ev_eng()
                wpe = em.op(eng, evac, waits=[(sP, wpm)] + list(ev_waits), tick=sem)
                bankg[bank_key] = (sem, wpe)
                return (sem, wpe), wpm

            wq_guard = None
            wp_q_mm_last = None
            grp_guard = []
            prev_exp = None
            prev_tr = None
            prev_av = None
            for g2 in range(n_groups):
                h0 = 2 * g2

                def ldwq(h, g2=g2):
                    return h.dma_start(
                        out=wqg_v,
                        in_=wq_d.rearrange("(k p) o -> p k o", p=128)[:, :, g2 * 384:(g2 + 1) * 384])
                wqw = [wq_guard] if wq_guard is not None else []
                wp_wq = em.op("gpsimd", ldwq, waits=wqw, tick=sWQ, tickk=16)

                kn_done = []
                for hh in range(2):
                    for kt in range(4):
                        bank = pb4 if (kt + hh) % 2 == 0 else pb5
                        bkey = "kv4" if (kt + hh) % 2 == 0 else "kv5"

                        def mms(h, hh=hh, kt=kt, bank=bank, h0=h0):
                            ins = None
                            for kc in range(4):
                                ins = h.matmul(
                                    out=bank[:, 0:512],
                                    lhsT=wkvb_4[:, kc, h0 + hh, 0:128],
                                    rhs=ckv_v[:, kc, kt * 512:(kt + 1) * 512],
                                    start=(kc == 0), stop=(kc == 3))
                            return ins

                        def evac(h, hh=hh, kt=kt, bank=bank, eng=None):
                            return None
                        eng, sem = ev_eng()

                        def evac2(h, hh=hh, kt=kt, bank=bank, eng=eng):
                            dst = knope_g[:, hh * 2048 + kt * 512: hh * 2048 + (kt + 1) * 512]
                            if eng == "vector":
                                return h.tensor_copy(out=dst, in_=bank[:, 0:512])
                            return h.copy(out=dst, in_=bank[:, 0:512])
                        d, _ = accum(bank, bkey, mms, evac2,
                                     [(sPL, 48)], grp_guard, eng=eng, sem=sem)
                        kn_done.append(d)

                v_done = []
                for tc in range(16):
                    bank = pb4 if tc % 2 == 0 else pb5
                    bkey = "kv4" if tc % 2 == 0 else "kv5"

                    def mms(h, tc=tc, bank=bank, h0=h0):
                        ins = None
                        for hh in range(2):
                            for kc in range(4):
                                ins = h.matmul(
                                    out=bank[:, hh * 128:(hh + 1) * 128],
                                    lhsT=ckv_v[:, kc, tc * 128:(tc + 1) * 128],
                                    rhs=wkvb_4[:, kc, h0 + hh, 128:256],
                                    start=(kc == 0), stop=(kc == 3))
                        return ins
                    eng, sem = ev_eng()

                    def evac2(h, tc=tc, bank=bank, eng=eng):
                        dst = v_g[:, tc * 256:(tc + 1) * 256]
                        if eng == "vector":
                            return h.tensor_copy(out=dst, in_=bank[:, 0:256])
                        return h.copy(out=dst, in_=bank[:, 0:256])
                    d, _ = accum(bank, bkey, mms, evac2, [(sPL, 48)], grp_guard, eng=eng, sem=sem)
                    v_done.append(d)

                q_done = []
                qpe_evs = []
                for hh in range(2):
                    for nb in range(2):
                        bank = pb4 if nb == 0 else pb5
                        bkey = "kv4" if nb == 0 else "kv5"

                        def mms(h, hh=hh, nb=nb, bank=bank):
                            ins = None
                            for dc in range(16):
                                ins = h.matmul(
                                    out=bank[:, 0:512],
                                    lhsT=wqg_v[:, dc, hh * 192: hh * 192 + 128],
                                    rhs=ln1m_v[:, dc, nb * 512:(nb + 1) * 512],
                                    start=(dc == 0), stop=(dc == 15))
                            return ins
                        eng, sem = ev_eng()

                        def evac2(h, hh=hh, nb=nb, bank=bank, eng=eng):
                            dst = qn_g[:, hh * 1024 + nb * 512: hh * 1024 + (nb + 1) * 512]
                            if eng == "vector":
                                return h.tensor_copy(out=dst, in_=bank[:, 0:512])
                            return h.copy(out=dst, in_=bank[:, 0:512])
                        d, wpm = accum(bank, bkey, mms, evac2, [(sWQ, wp_wq)], grp_guard, eng=eng, sem=sem)
                        q_done.append(d)
                        wp_q_mm_last = wpm

                        def mms_pe(h, hh=hh, nb=nb, bank=bank):
                            ins = None
                            for dc in range(16):
                                ins = h.matmul(
                                    out=bank[0:64, 0:512],
                                    lhsT=wqg_v[:, dc, hh * 192 + 128: hh * 192 + 192],
                                    rhs=ln1m_v[:, dc, nb * 512:(nb + 1) * 512],
                                    start=(dc == 0), stop=(dc == 15))
                            return ins

                        def evac_pe(h, hh=hh, nb=nb, bank=bank):
                            h.tensor_copy(out=qab[hh * 32:hh * 32 + 32, nb * 512:(nb + 1) * 512],
                                          in_=bank[0:32, 0:512])
                            return h.tensor_copy(out=qab[hh * 32:hh * 32 + 32, 1024 + nb * 512: 1024 + (nb + 1) * 512],
                                                 in_=bank[32:64, 0:512])
                        d, wpm = accum(bank, bkey, mms_pe, evac_pe, [(sWQ, wp_wq)], grp_guard,
                                       eng="vector", sem=sV)
                        qpe_evs.append(d)
                        wp_q_mm_last = wpm
                wq_guard = (sP, wp_q_mm_last)

                def qp1(h):
                    h.tensor_mul(out=qrw[:, 0:1024], in0=qab[:, 0:1024], in1=cq64[:, :])
                    return h.tensor_mul(out=qrw[:, 1024:2048], in0=qab[:, 1024:2048], in1=sq64[:, :])
                wp_a = em.op("vector", qp1, waits=qpe_evs + [(sQ, wp_tab)] + grp_guard, tick=sV)

                def qp2(h):
                    ins = None
                    for hh in range(2):
                        ins = h.tensor_sub(
                            out=qpe_g[0:32, hh * 1024:(hh + 1) * 1024],
                            in0=qrw[hh * 32:hh * 32 + 32, 0:1024],
                            in1=qrw[hh * 32:hh * 32 + 32, 1024:2048])
                    return ins
                wp_b = em.op("vector", qp2, waits=[(sV, wp_a)], tick=sV)

                def qp3(h):
                    h.tensor_mul(out=qrw[:, 0:1024], in0=qab[:, 0:1024], in1=sq64[:, :])
                    return h.tensor_mul(out=qrw[:, 1024:2048], in0=qab[:, 1024:2048], in1=cq64[:, :])
                wp_c = em.op("vector", qp3, waits=[(sV, wp_b)], tick=sV)

                def qp4(h):
                    ins = None
                    for hh in range(2):
                        ins = h.tensor_add(
                            out=qpe_g[32:64, hh * 1024:(hh + 1) * 1024],
                            in0=qrw[hh * 32:hh * 32 + 32, 1024:2048],
                            in1=qrw[hh * 32:hh * 32 + 32, 0:1024])
                    return ins
                wp_qf = em.op("vector", qp4, waits=[(sV, wp_c)], tick=sV)
                q_ready = [(sV, wp_qf)] + q_done

                tflip = [0]
                for hh in range(2):
                    hglob = h0 + hh
                    for i in range(n_slots):
                        klen = (2 * i + 2) * 128
                        nblk = (klen + 511) // 512
                        sc_ev = []
                        for kb in range(nblk):
                            w0 = kb * 512
                            wid = min(512, klen - w0)
                            bank = pb0 if kb % 2 == 0 else pb1
                            bkey = "s0" if kb % 2 == 0 else "s1"

                            def mms(h, hh=hh, i=i, w0=w0, wid=wid, bank=bank):
                                h.matmul(out=bank[:, 0:wid],
                                         lhsT=qn_g[:, hh * 1024 + i * 128:(hh * 1024 + (i + 1) * 128)],
                                         rhs=knope_g[:, hh * 2048 + w0: hh * 2048 + w0 + wid],
                                         start=True, stop=False)
                                return h.matmul(out=bank[:, 0:wid],
                                                lhsT=qpe_g[:, hh * 1024 + i * 128:(hh * 1024 + (i + 1) * 128)],
                                                rhs=kpe[:, w0:w0 + wid],
                                                start=False, stop=True)

                            last = (kb == nblk - 1)
                            eng, sem = ("vector", sV)

                            def evac2(h, w0=w0, wid=wid, bank=bank, last=last, klen=klen):
                                if not last:
                                    return h.tensor_copy(out=srow[:, w0:w0 + wid], in_=bank[:, 0:wid])
                                if wid > 256:
                                    h.tensor_copy(out=srow[:, w0:w0 + wid - 256], in_=bank[:, 0:wid - 256])
                                return h.tensor_tensor(out=srow[:, klen - 256:klen],
                                                       in0=bank[:, wid - 256:wid],
                                                       in1=qmk[:, :], op=ALU.add)
                            ev_extra = [prev_exp] if prev_exp is not None else []
                            d, _ = accum(bank, bkey, mms, evac2,
                                         q_ready + kn_done + [(sQ, wp_tab)], ev_extra,
                                         eng=eng, sem=sem)
                            sc_ev.append(d)

                        wp_mx = em.op("vector", lambda h, klen=klen: h.reduce_max(
                            out=cols[:, 4:5], in_=srow[:, 0:klen], axis=AX.X),
                            waits=sc_ev, tick=sV)
                        wp_ns = em.op("vector", lambda h: h.tensor_scalar(
                            out=cols[:, 5:6], in0=cols[:, 4:5], scalar1=-SCALE, scalar2=None,
                            op0=ALU.mult),
                            waits=[(sV, wp_mx)] + ([prev_exp] if prev_exp else []), tick=sV)
                        wp_ex = em.op("scalar", lambda h, klen=klen: h.activation(
                            out=abf_r[:, 0:klen], in_=srow[:, 0:klen], func=AF.Exp,
                            scale=SCALE, bias=cols[:, 5:6], accum_out=cols[:, 6:7]),
                            waits=[(sV, wp_ns)] + ([prev_tr] if prev_tr else []), tick=sA)
                        prev_exp = (sA, wp_ex)
                        wp_rc2 = em.op("vector", lambda h: h.reciprocal(
                            out=cols[:, 7:8], in_=cols[:, 6:7]),
                            waits=[(sA, wp_ex)], tick=sV)
                        wp_nm = em.op("vector", lambda h, klen=klen: h.tensor_scalar(
                            out=anm_r[:, 0:klen], in0=abf_r[:, 0:klen],
                            scalar1=cols[:, 7:8], scalar2=None, op0=ALU.mult),
                            waits=[(sV, wp_rc2)], tick=sV)

                        nt = klen // 128
                        at_ev = []
                        for t0 in range(0, nt, 8):
                            cnt = min(8, nt - t0)
                            half = tflip[0] % 2
                            tflip[0] += 1
                            bkey = "tpA" if half == 0 else "tpB"
                            w = [(sV, wp_nm)]
                            if bkey in bankg:
                                w.append(bankg[bkey])

                            def mms(h, t0=t0, cnt=cnt, half=half):
                                ins = None
                                for j in range(cnt):
                                    ins = h.transpose(
                                        out=tp[half][:, j * 128:(j + 1) * 128],
                                        in_=anm_r[:, (t0 + j) * 128:(t0 + j + 1) * 128],
                                        identity=id_bf[:, :])
                                return ins
                            wpm = em.op("tensor", mms, waits=w, tick=sP)
                            prev_tr = (sP, wpm)
                            eng, sem = ev_eng()

                            def evac2(h, t0=t0, cnt=cnt, half=half, eng=eng):
                                dst = atT[:, t0 * 128:(t0 + cnt) * 128]
                                src = tp[half][:, 0:cnt * 128]
                                if eng == "vector":
                                    return h.tensor_copy(out=dst, in_=src)
                                return h.copy(out=dst, in_=src)
                            wpe = em.op(eng, evac2,
                                        waits=[(sP, wpm)] + ([prev_av] if prev_av else []),
                                        tick=sem)
                            bankg[bkey] = (sem, wpe)
                            at_ev.append((sem, wpe))

                        bank = pb2 if (i + hh) % 2 == 0 else pb3
                        bkey = "o2" if (i + hh) % 2 == 0 else "o3"

                        def mms(h, hh=hh, nt=nt, bank=bank):
                            ins = None
                            for t in range(nt):
                                ins = h.matmul(
                                    out=bank[:, 0:128],
                                    lhsT=v_g[:, t * 256 + hh * 128: t * 256 + (hh + 1) * 128],
                                    rhs=atT[:, t * 128:(t + 1) * 128],
                                    start=(t == 0), stop=(t == nt - 1))
                            return ins
                        eng, sem = ev_eng()

                        def evac2(h, hglob=hglob, i=i, bank=bank, eng=eng):
                            dst = o_fm[:, hglob * 1024 + i * 128: hglob * 1024 + (i + 1) * 128]
                            if eng == "vector":
                                return h.tensor_copy(out=dst, in_=bank[:, 0:128])
                            return h.copy(out=dst, in_=bank[:, 0:128])
                        d, wpm_av = accum(bank, bkey, mms, evac2, at_ev + v_done, [],
                                          eng=eng, sem=sem)
                        prev_av = (sP, wpm_av)
                grp_guard = [prev_av]


        if stage >= 3:
            em.barrier(ALL_SEMS)
            align = lambda x: (x + 63) // 64 * 64
            yacc = ar.at(align(att_base), 128, 16 * TPC, F32)
            spare_att = align(att_base) + 16 * TPC * 4
            ov = align(p12_base)
            ln2 = ar.at(ov, 128, 16 * TPC, BF16)
            o_xt2 = align(ov + 32 * 1024)
            xt2 = [ar.at(o_xt2 + i * 8192, 128, D, F32) for i in range(2)]
            o_woc = o_xt2 + 16 * 1024
            wo_c = [ar.at(o_woc + i * 4096, 128, 16 * 128, BF16) for i in range(2)]
            o_sqx = o_woc + 8 * 1024
            sqx = ar.at(o_sqx, 128, 4 * 512, BF16)
            o_g = o_sqx + 4 * 1024
            gw_sb = ar.at(o_g, 16, D, BF16)
            gwT = ar.at(o_g + 4096, 128, 16 * 16, BF16)
            sg_sb = ar.at(o_g + 4608, 16, TPC, F32)
            sc_tm = ar.at(o_g + 8704, 128, 8 * 16, F32)
            m8t = ar.at(o_g + 9280, 128, 8, F32)
            dw_tm = ar.at(o_g + 9344, 128, 8 * 16, F32)
            row5 = ar.at(o_g + 9920, 1, 6 * 512, F32)
            dwT = ar.at(align(ov + 114 * 1024), 16, 8 * 128, BF16)
            assert o_g + 9920 + 6 * 512 * 4 <= p12_base + 83 * 1024

            yacc_v = r3(yacc, 16)
            ln2_v = r3(ln2, 16)
            ofm_v = r3(o_fm, 16)

            xg = [None, None]
            xev = []
            for cm in range(NSLOT):
                par = cm % 2

                def ldx(h, cm=cm, par=par):
                    return h.dma_start(out=xt2[par][:, :], in_=xm_t[cm, :, :])
                wp_x = em.op("gpsimd", ldx, waits=([xg[par]] if xg[par] else []), tick=sX[par], tickk=16)
                for g in range(4):
                    bank = pb0 if (cm * 4 + g) % 2 == 0 else pb1
                    bkey = "s0" if (cm * 4 + g) % 2 == 0 else "s1"
                    w = [(sX[par], wp_x), (sQ, wp_consts)]
                    if bkey in bankg:
                        w.append(bankg[bkey])

                    def mms(h, g=g, par=par, bank=bank):
                        ins = None
                        for j in range(4):
                            dc = 4 * g + j
                            ins = h.transpose(out=bank[:, j * 128:(j + 1) * 128],
                                              in_=xt2[par][:, dc * 128:(dc + 1) * 128],
                                              identity=id_f[:, :])
                        return ins
                    wpm = em.op("tensor", mms, waits=w, tick=sP)
                    eng, sem = ev_eng()

                    def evac2(h, g=g, cm=cm, bank=bank, eng=eng):
                        dst = yacc_v[:, 4 * g:4 * g + 4, cm * 128:(cm + 1) * 128]
                        src = bank.rearrange("p (j m) -> p j m", j=4)
                        if eng == "vector":
                            return h.tensor_copy(out=dst, in_=src)
                        return h.copy(out=dst, in_=src)
                    wpe = em.op(eng, evac2, waits=[(sP, wpm)], tick=sem)
                    bankg[bkey] = (sem, wpe)
                    xev.append((sem, wpe))
                    xg[par] = (sP, wpm)
            x_done = [xev[-2], xev[-1], xev[-3], xev[-4]]

            wog = [None, None]
            wo_r = wo_d.rearrange("(k p) c -> p k c", p=128)
            wo_done = []
            for oc in range(16):
                par = oc % 2

                def ldw(h, oc=oc, par=par):
                    return h.dma_start(out=r3(wo_c[par], 16),
                                      in_=wo_r[:, :, oc * 128:(oc + 1) * 128])
                wp_w = em.op("gpsimd", ldw, waits=([wog[par]] if wog[par] else []), tick=sWO[par], tickk=16)
                for nb in range(2):
                    bank = pb2 if (oc * 2 + nb) % 2 == 0 else pb3
                    bkey = "o2" if (oc * 2 + nb) % 2 == 0 else "o3"
                    w = [(sWO[par], wp_w)]
                    if bkey in bankg:
                        w.append(bankg[bkey])

                    def mms(h, oc=oc, nb=nb, par=par, bank=bank):
                        ins = None
                        for k in range(16):
                            ins = h.matmul(out=bank[:, 0:512],
                                           lhsT=wo_c[par][:, k * 128:(k + 1) * 128],
                                           rhs=ofm_v[:, k, nb * 512:(nb + 1) * 512],
                                           start=(k == 0), stop=(k == 15))
                        return ins
                    wpm = em.op("tensor", mms, waits=w, tick=sP)
                    wog[par] = (sP, wpm)

                    def evac2(h, oc=oc, nb=nb, bank=bank):
                        dst = yacc_v[:, oc, nb * 512:(nb + 1) * 512]
                        return h.scalar_tensor_tensor(out=dst, in0=bank[:, 0:512],
                                                      scalar=1.0, in1=dst,
                                                      op0=ALU.mult, op1=ALU.add)
                    wpe = em.op("vector", evac2, waits=[(sP, wpm)] + xev, tick=sV)
                    bankg[bkey] = (sV, wpe)
                    wo_done.append((sV, wpe))
            x2_done = wo_done[-4:]

            ln2_done = []
            g_sq = {}
            last_nrm = None
            for nb in range(2):
                wpx = None
                wpq = None
                for dc in range(16):
                    slot_b = (dc % 2)
                    slot_s = 2 + (dc % 2)
                    wb = [w for w in [g_sq.get(slot_b)] if w] + x2_done
                    if last_nrm is not None:
                        wb.append(last_nrm)
                    wp_cb = em.op("vector", lambda h, dc=dc, nb=nb, slot_b=slot_b: h.tensor_copy(
                        out=r3(sqx, 4)[:, slot_b, :],
                        in_=yacc_v[:, dc, nb * 512:(nb + 1) * 512]),
                        waits=wb, tick=sV)
                    ws = [w for w in [g_sq.get(slot_s)] if w] + x2_done
                    wp_sq2 = em.op("scalar", lambda h, dc=dc, nb=nb, slot_s=slot_s: h.activation(
                        out=r3(sqx, 4)[:, slot_s, :],
                        in_=yacc_v[:, dc, nb * 512:(nb + 1) * 512], func=AF.Square),
                        waits=ws, tick=sA)

                    def mmx(h, dc=dc, slot_b=slot_b):
                        return h.matmul(out=pb4[0:1, 0:512], lhsT=ones_col[:, :],
                                        rhs=r3(sqx, 4)[:, slot_b, :],
                                        start=(dc == 0), stop=(dc == 15))
                    wq_ = [(sV, wp_cb)]
                    if dc == 0 and "st4" in bankg:
                        wq_.append(bankg["st4"])
                    wpx = em.op("tensor", mmx, waits=wq_, tick=sP)

                    def mmq(h, dc=dc, slot_s=slot_s):
                        return h.matmul(out=pb5[0:1, 0:512], lhsT=ones_col[:, :],
                                        rhs=r3(sqx, 4)[:, slot_s, :],
                                        start=(dc == 0), stop=(dc == 15))
                    wq2_ = [(sA, wp_sq2)]
                    if dc == 0 and "st5" in bankg:
                        wq2_.append(bankg["st5"])
                    wpq = em.op("tensor", mmq, waits=wq2_, tick=sP)
                    g_sq[slot_b] = (sP, wpx)
                    g_sq[slot_s] = (sP, wpq)

                rv = r3(row5, 6)
                wp_m = em.op("vector", lambda h: h.tensor_scalar(
                    out=rv[0:1, 0, :], in0=pb4[0:1, 0:512], scalar1=1.0 / D, scalar2=None,
                    op0=ALU.mult), waits=[(sP, wpx)], tick=sV)
                wp_m2 = em.op("vector", lambda h: h.tensor_mul(
                    out=rv[0:1, 1, :], in0=rv[0:1, 0, :], in1=rv[0:1, 0, :]),
                    waits=[(sV, wp_m)], tick=sV)
                wp_var = em.op("vector", lambda h: h.scalar_tensor_tensor(
                    out=rv[0:1, 2, :], in0=pb5[0:1, 0:512], scalar=1.0 / D,
                    in1=rv[0:1, 1, :], op0=ALU.mult, op1=ALU.subtract),
                    waits=[(sV, wp_m2), (sP, wpq)], tick=sV)
                bankg["st4"] = (sV, wp_var)
                bankg["st5"] = (sV, wp_var)
                wp_sd = em.op("scalar", lambda h: h.activation(
                    out=rv[0:1, 3, :], in_=rv[0:1, 2, :], func=AF.Sqrt, bias=epsc[0:1, 0:1]),
                    waits=[(sV, wp_var)], tick=sA)
                wp_rs = em.op("vector", lambda h: h.reciprocal(
                    out=rv[0:1, 4, :], in_=rv[0:1, 3, :]), waits=[(sA, wp_sd)], tick=sV)
                wp_ms = em.op("vector", lambda h: h.tensor_mul(
                    out=rv[0:1, 5, :], in0=rv[0:1, 0, :], in1=rv[0:1, 4, :]),
                    waits=[(sV, wp_rs)], tick=sV)
                wrs = [(sV, wp_ms)]
                if "s0" in bankg:
                    wrs.append(bankg["s0"])
                wp_b1 = em.op("tensor", lambda h: h.matmul(
                    out=pb0[:, 0:512], lhsT=ones_row[:, :], rhs=rv[0:1, 4, :],
                    start=True, stop=True), waits=wrs, tick=sP)
                wms = [(sV, wp_ms)]
                if "s1" in bankg:
                    wms.append(bankg["s1"])
                wp_b2 = em.op("tensor", lambda h: h.matmul(
                    out=pb1[:, 0:512], lhsT=ones_row[:, :], rhs=rv[0:1, 5, :],
                    start=True, stop=True), waits=wms, tick=sP)
                nrm_done = []
                prev_nrm = None
                for dc in range(16):
                    def nrm1(h, dc=dc, nb=nb):
                        return h.tensor_tensor(out=r3(sqx, 4)[:, 0, :],
                                               in0=yacc_v[:, dc, nb * 512:(nb + 1) * 512],
                                               in1=pb0[:, 0:512], op=ALU.mult)
                    nw = [(sP, wp_b1), (sP, wp_b2)]
                    if prev_nrm is not None:
                        nw.append(prev_nrm)
                    wpn1 = em.op("vector", nrm1, waits=nw, tick=sV)

                    def nrm2(h, dc=dc, nb=nb):
                        return h.tensor_tensor(out=ln2_v[:, dc, nb * 512:(nb + 1) * 512],
                                               in0=r3(sqx, 4)[:, 0, :],
                                               in1=pb1[:, 0:512], op=ALU.subtract)
                    wpn = em.op("vector", nrm2, waits=[(sV, wpn1)], tick=sV)
                    prev_nrm = (sV, wpn)
                    nrm_done.append((sV, wpn))
                last_nrm = prev_nrm
                bankg["s0"] = nrm_done[-1]
                bankg["s1"] = nrm_done[-1]
                ln2_done += nrm_done

            wp_gw = em.op("gpsimd", lambda h: h.dma_start(out=gw_sb[:, :], in_=gw_d),
                          tick=sGW, tickk=16)
            gt_w = [(sGW, wp_gw), (sQ, wp_consts)]
            if "tpA" in bankg:
                gt_w.append(bankg["tpA"])

            def gw_tr(h):
                ins = None
                for dc in range(16):
                    ins = h.transpose(out=tpA[:, dc * 16:(dc + 1) * 16],
                                      in_=gw_sb[:, dc * 128:(dc + 1) * 128],
                                      identity=id_bf[0:16, 0:16])
                return ins
            wp_gt = em.op("tensor", gw_tr, waits=gt_w, tick=sP)
            wp_ge = em.op("vector", lambda h: h.tensor_copy(out=gwT[:, :], in_=tpA[:, 0:256]),
                          waits=[(sP, wp_gt)], tick=sV)
            bankg["tpA"] = (sV, wp_ge)

            sig_done = []
            for nb2 in range(2):
                bank = pb2 if nb2 == 0 else pb3
                bkey = "o2" if nb2 == 0 else "o3"

                def mms(h, nb2=nb2, bank=bank):
                    ins = None
                    for dc in range(16):
                        ins = h.matmul(out=bank[0:16, 0:512],
                                       lhsT=gwT[:, dc * 16:(dc + 1) * 16],
                                       rhs=ln2_v[:, dc, nb2 * 512:(nb2 + 1) * 512],
                                       start=(dc == 0), stop=(dc == 15))
                    return ins
                w = [(sV, wp_ge)] + ln2_done
                if bkey in bankg:
                    w.append(bankg[bkey])
                wpm = em.op("tensor", mms, waits=w, tick=sP)

                def sig(h, nb2=nb2, bank=bank):
                    return h.activation(out=sg_sb[:, nb2 * 512:(nb2 + 1) * 512],
                                        in_=bank[0:16, 0:512], func=AF.Sigmoid)
                wpe = em.op("scalar", sig, waits=[(sP, wpm)], tick=sA)
                bankg[bkey] = (sA, wpe)
                sig_done.append((sA, wpe))

            def sig_tr(h):
                ins = None
                for tc in range(8):
                    ins = h.transpose(out=pb3[:, tc * 16:(tc + 1) * 16],
                                      in_=sg_sb[:, tc * 128:(tc + 1) * 128],
                                      identity=id_f[0:16, 0:16])
                return ins
            w = sig_done + [bankg["o3"]]
            wp_st = em.op("tensor", sig_tr, waits=w, tick=sP)
            wp_se = em.op("vector", lambda h: h.tensor_copy(out=sc_tm[:, :], in_=pb3[:, 0:128]),
                          waits=[(sP, wp_st)], tick=sV)
            bankg["o3"] = (sV, wp_se)

            wp_dw = wp_se
            for tc in range(8):
                csl = lambda a, b: sc_tm[:, tc * 16 + a: tc * 16 + b]
                wp_m8 = em.op("vector", lambda h, tc=tc: h.max(
                    out=m8t[:, :], in_=sc_tm[:, tc * 16:(tc + 1) * 16]),
                    waits=[(sV, wp_dw)], tick=sV)
                wp_s = em.op("vector", lambda h: h.tensor_add(
                    out=cols[:, 8:9], in0=m8t[:, 0:1], in1=m8t[:, 1:2]),
                    waits=[(sV, wp_m8)], tick=sV)
                wp_r = em.op("vector", lambda h: h.reciprocal(
                    out=cols[:, 9:10], in_=cols[:, 8:9]), waits=[(sV, wp_s)], tick=sV)
                wp_w1 = em.op("vector", lambda h: h.tensor_mul(
                    out=cols[:, 10:11], in0=m8t[:, 0:1], in1=cols[:, 9:10]),
                    waits=[(sV, wp_r)], tick=sV)
                wp_w2 = em.op("vector", lambda h: h.tensor_mul(
                    out=cols[:, 11:12], in0=m8t[:, 1:2], in1=cols[:, 9:10]),
                    waits=[(sV, wp_w1)], tick=sV)
                wp_e1 = em.op("vector", lambda h, tc=tc: h.tensor_scalar(
                    out=dw_tm[:, tc * 16:(tc + 1) * 16], in0=sc_tm[:, tc * 16:(tc + 1) * 16],
                    scalar1=m8t[:, 0:1], scalar2=cols[:, 10:11],
                    op0=ALU.is_equal, op1=ALU.mult), waits=[(sV, wp_w2)], tick=sV)
                wp_e2 = em.op("vector", lambda h, tc=tc: h.tensor_scalar(
                    out=sc_tm[:, tc * 16:(tc + 1) * 16], in0=sc_tm[:, tc * 16:(tc + 1) * 16],
                    scalar1=m8t[:, 1:2], scalar2=cols[:, 11:12],
                    op0=ALU.is_equal, op1=ALU.mult), waits=[(sV, wp_e1)], tick=sV)
                wp_dw = em.op("vector", lambda h, tc=tc: h.tensor_add(
                    out=dw_tm[:, tc * 16:(tc + 1) * 16], in0=dw_tm[:, tc * 16:(tc + 1) * 16],
                    in1=sc_tm[:, tc * 16:(tc + 1) * 16]),
                    waits=[(sV, wp_e2)], tick=sV)

            def dw_tr(h):
                ins = None
                for tc in range(8):
                    bank = pb4 if tc < 4 else pb5
                    ins = h.transpose(out=bank[0:16, (tc % 4) * 128:((tc % 4) + 1) * 128],
                                      in_=dw_tm[:, tc * 16:(tc + 1) * 16],
                                      identity=id_f[:, :])
                return ins
            w = [(sV, wp_dw), bankg["st4"], bankg["st5"]]
            wp_dt = em.op("tensor", dw_tr, waits=w, tick=sP)

            def dw_ev(h):
                h.tensor_copy(out=dwT[:, 0:512], in_=pb4[0:16, 0:512])
                return h.tensor_copy(out=dwT[:, 512:1024], in_=pb5[0:16, 0:512])
            wp_de = em.op("vector", dw_ev, waits=[(sP, wp_dt)], tick=sV)
            bankg["st4"] = (sV, wp_de)
            bankg["st5"] = (sV, wp_de)
            dw_ready = (sV, wp_de)


        if stage >= 4:
            em.barrier(ALL_SEMS)
            ov2 = align(p12_base)
            h_sb = ar.at(ov2 + 32 * 1024, 128, 11 * 1024, BF16)
            w13 = ar.at(ov2 + 54 * 1024, 128, 3 * 2 * 16 * 128, BF16)
            w2a = ar.at(ov2 + 78 * 1024, 128, 6 * 2048, BF16)
            dwb = ar.at(ov2 + 102 * 1024, 128, TPC, BF16)
            ytmp = ar.at(ov2 + 104 * 1024, 128, 2 * 512, BF16)
            w2b = ar.at(spare_att, 128, 5 * 2048, BF16)
            sqx6 = ar.at(ov2 + 106 * 1024, 128, 4 * 512, BF16)
            assert ar.hi <= 207 * 1024, ar.hi

            w13_v = w13.rearrange("p (b w k m) -> p b w k m", b=3, w=2, k=16)
            h_v = r3(h_sb, 11)

            NFC = 11
            ranges = [(fc * 128, 128) for fc in range(NFC)]

            def w2strip(fc):
                return (w2a, fc) if fc < 6 else (w2b, fc - 6)

            w13_guard = {}
            sx_guard = {}
            w2_guard = None
            dwb_guard = None
            mm2_last = None
            h_guard = {}

            it_list = list(range(n_iters)) if IT_LIST is None else IT_LIST
            for it in it_list:
                if IT_BARRIER and it > 0:
                    em.barrier(ALL_SEMS)
                shared = it < 2
                if not shared:
                    e = it - 2

                def ldw2a(h, it=it):
                    return h.dma_start(out=w2a[:, :], in_=w2t_d[it, :, 0:6 * 2048])
                w = [w2_guard] if w2_guard is not None else []
                em.op("sync", ldw2a, waits=w, tick=sW2, tickk=16)

                def ldw2b(h, it=it):
                    return h.dma_start(out=w2b[:, :], in_=w2t_d[it, :, 6 * 2048:11 * 2048])
                em.op("sync", ldw2b, waits=w, tick=sW2, tickk=16)
                wp_w2all = sW2.n

                if not shared:
                    dwb_w = [dw_ready]
                    if dwb_guard is not None:
                        dwb_w.append(dwb_guard)
                    if "y4" in bankg:
                        dwb_w.append(bankg["y4"])
                    if "y5" in bankg:
                        dwb_w.append(bankg["y5"])

                    def bc1(h, e=e):
                        return h.matmul(out=pb4[:, 0:512], lhsT=oh_sb[:, e * 128:(e + 1) * 128],
                                        rhs=dwT[:, 0:512], start=True, stop=True)
                    wpb1 = em.op("tensor", bc1, waits=dwb_w, tick=sP)

                    def bc2(h, e=e):
                        return h.matmul(out=pb5[:, 0:512], lhsT=oh_sb[:, e * 128:(e + 1) * 128],
                                        rhs=dwT[:, 512:1024], start=True, stop=True)
                    wpb2 = em.op("tensor", bc2, waits=[], tick=sP)

                    def bce(h):
                        h.copy(out=dwb[:, 0:512], in_=pb4[:, 0:512])
                        return h.copy(out=dwb[:, 512:1024], in_=pb5[:, 0:512])
                    wpbe = em.op("scalar", bce, waits=[(sP, wpb1), (sP, wpb2)], tick=sA)
                    bankg["y4"] = (sA, wpbe)
                    bankg["y5"] = (sA, wpbe)
                    dwb_ready = (sA, wpbe)

                h_done = {}
                for ri, (r0, wd) in enumerate(ranges):
                    buf = (it * NFC + ri) % 3
                    wg = [w13_guard[buf]] if buf in w13_guard else []

                    def ld13(h, it=it, ri=ri, buf=buf):
                        return h.dma_start(out=w13[:, buf * 4096:(buf + 1) * 4096],
                                           in_=w13_d[it, ri])
                    wp3 = em.op("sync", ld13, waits=wg, tick=sW13[buf], tickk=16)

                    nfb = 1
                    for fb in range(nfb):
                        fc = ri
                        for th in range(2):
                            ab = pb0 if th == 0 else pb1
                            abk = "a0" if th == 0 else "a1"
                            bb = pb2 if th == 0 else pb3
                            bbk = "b0" if th == 0 else "b1"

                            def mma(h, buf=buf, fb=fb, th=th, ab=ab):
                                ins = None
                                for dc in range(16):
                                    ins = h.matmul(
                                        out=ab[:, 0:512],
                                        lhsT=w13_v[:, buf, 0, dc, :],
                                        rhs=ln2_v[:, dc, th * 512:(th + 1) * 512],
                                        start=(dc == 0), stop=(dc == 15))
                                return ins
                            wa = [(sW13[buf], wp3)] + ln2_done
                            if abk in bankg:
                                wa.append(bankg[abk])
                            if fc in h_guard:
                                wa.append(h_guard[fc])
                            wpa = em.op("tensor", mma, waits=wa, tick=sP)

                            def mmb(h, buf=buf, fb=fb, th=th, bb=bb):
                                ins = None
                                for dc in range(16):
                                    ins = h.matmul(
                                        out=bb[:, 0:512],
                                        lhsT=w13_v[:, buf, 1, dc, :],
                                        rhs=ln2_v[:, dc, th * 512:(th + 1) * 512],
                                        start=(dc == 0), stop=(dc == 15))
                                return ins
                            wb = [(sW13[buf], wp3)]
                            if bbk in bankg:
                                wb.append(bankg[bbk])
                            wpb = em.op("tensor", mmb, waits=wb, tick=sP)
                            if ri == len(ranges) - 1 and fb == nfb - 1 and th == 1:
                                w13_guard[buf] = (sP, wpb)
                            else:
                                w13_guard[buf] = (sP, wpb)

                            def sil(h, th=th, ab=ab):
                                return h.activation(out=r3(sqx6, 4)[:, th, :],
                                                    in_=ab[:, 0:512], func=AF.Silu)
                            sw_ = [(sP, wpa)]
                            if th in sx_guard:
                                sw_.append(sx_guard[th])
                            wps = em.op("scalar", sil, waits=sw_, tick=sA)
                            bankg[abk] = (sA, wps)

                            def bcp(h, th=th, bb=bb):
                                return h.copy(out=r3(sqx6, 4)[:, 2 + th, :], in_=bb[:, 0:512])
                            bw_ = [(sP, wpb)]
                            if th in sx_guard:
                                bw_.append(sx_guard[th])
                            wpc = em.op("scalar", bcp, waits=bw_, tick=sA)
                            bankg[bbk] = (sA, wpc)

                            def hm(h, fc=fc, th=th):
                                if DBG_HM == "sil":
                                    return h.tensor_copy(
                                        out=h_v[:, fc, th * 512:(th + 1) * 512],
                                        in_=r3(sqx6, 4)[:, th, :])
                                if DBG_HM == "b":
                                    return h.tensor_copy(
                                        out=h_v[:, fc, th * 512:(th + 1) * 512],
                                        in_=r3(sqx6, 4)[:, 2 + th, :])
                                return h.tensor_mul(
                                    out=h_v[:, fc, th * 512:(th + 1) * 512],
                                    in0=r3(sqx6, 4)[:, th, :], in1=r3(sqx6, 4)[:, 2 + th, :])
                            hw = [(sA, wps), (sA, wpc)]
                            if mm2_last is not None:
                                hw.append(mm2_last)
                            wph = em.op("vector", hm, waits=hw, tick=sV)
                            h_done[(fc, th)] = (sV, wph)
                            sx_guard[th] = (sV, wph)

                all_h = list(h_done.values())
                for c in range(16):
                    for th in range(2):
                        yb = pb4 if (c * 2 + th) % 2 == 0 else pb5
                        ybk = "y4" if (c * 2 + th) % 2 == 0 else "y5"

                        def mm2(h, c=c, th=th, yb=yb):
                            ins = None
                            for fc in range(NFC):
                                t2, si = w2strip(fc)
                                ins = h.matmul(
                                    out=yb[:, 0:512],
                                    lhsT=t2[:, si * 2048 + c * 128: si * 2048 + (c + 1) * 128],
                                    rhs=h_v[:, fc, th * 512:(th + 1) * 512],
                                    start=(fc == 0), stop=(fc == NFC - 1))
                            return ins
                        wm = all_h + [(sW2, wp_w2all)]
                        if ybk in bankg:
                            wm.append(bankg[ybk])
                        wpm = em.op("tensor", mm2, waits=wm, tick=sP)
                        mm2_last = (sP, wpm)

                        if shared:
                            def yadd(h, c=c, th=th, yb=yb):
                                dst = yacc_v[:, c, th * 512:(th + 1) * 512]
                                return h.scalar_tensor_tensor(
                                    out=dst, in0=yb[:, 0:512], scalar=1.0, in1=dst,
                                    op0=ALU.mult, op1=ALU.add)
                            wpy = em.op("vector", yadd, waits=[(sP, wpm)], tick=sV)
                        else:
                            def ysc(h, th=th, yb=yb):
                                return h.tensor_tensor(
                                    out=r3(ytmp, 2)[:, th, :], in0=yb[:, 0:512],
                                    in1=dwb[:, th * 512:(th + 1) * 512], op=ALU.mult)
                            wpy0 = em.op("vector", ysc, waits=[(sP, wpm), dwb_ready], tick=sV)

                            def yad(h, c=c, th=th):
                                dst = yacc_v[:, c, th * 512:(th + 1) * 512]
                                return h.tensor_add(out=dst, in0=r3(ytmp, 2)[:, th, :], in1=dst)
                            wpy = em.op("vector", yad, waits=[(sV, wpy0)], tick=sV)
                            dwb_guard = (sV, wpy0)
                        bankg[ybk] = (sV, wpy)
                w2_guard = mm2_last
                for fc in range(NFC):
                    h_guard[fc] = mm2_last

            if dbg_moe:
                em.barrier([sP, sV, sA], engines=("sync",))
                em.op("sync", lambda h: h.dma_start(out=taps["t_h"], in_=h_sb[:, :]), tick=sQ, tickk=16)
                em.op("sync", lambda h: h.dma_start(out=taps["t_w13"], in_=w13[:, :]), tick=sQ, tickk=16)
                em.op("sync", lambda h: h.dma_start(out=taps["t_w2a"], in_=w2a[:, :]), tick=sQ, tickk=16)
                em.op("sync", lambda h: h.dma_start(out=taps["t_ln2b"], in_=ln2[:, :]), tick=sQ, tickk=16)
                em.wait("sync", [(sQ, sQ.n)])
            em.barrier([sP, sV, sA], engines=("gpsimd",))
            em.op("gpsimd", lambda h: h.dma_start(
                out=y_d.rearrange("(k p) t -> p k t", p=128), in_=yacc_v),
                tick=sO, tickk=16)
            em.wait("gpsimd", [(sO, sO.n)])
            em.wait("sync", [(sO, sO.n)])

        em.flush()

        if stage <= 1:
            em.barrier([sP, sV, sA], engines=("sync",))
            em.op("sync", lambda h: h.dma_start(out=taps["t_ln1m"], in_=ln1m[:, :]), tick=sQ, tickk=16)
            if stage == 1:
                em.op("sync", lambda h: h.dma_start(out=taps["t_ckv"], in_=ckv[:, :]), tick=sQ, tickk=16)
                em.op("sync", lambda h: h.dma_start(out=taps["t_kpe"], in_=kpe[:, :]), tick=sQ, tickk=16)
            em.wait("sync", [(sQ, sQ.n)])
            em.flush()
        if stage == 3:
            em.barrier([sP, sV, sA], engines=("sync",))
            em.op("sync", lambda h: h.dma_start(out=taps["t_yacc"], in_=yacc[:, :]), tick=sQ, tickk=16)
            em.op("sync", lambda h: h.dma_start(out=taps["t_ln2"], in_=ln2[:, :]), tick=sQ, tickk=16)
            em.op("sync", lambda h: h.dma_start(out=taps["t_dw"], in_=dw_tm[:, :]), tick=sQ, tickk=16)
            em.op("sync", lambda h: h.dma_start(out=taps["t_dwT"], in_=dwT[:, :]), tick=sQ, tickk=16)
            em.wait("sync", [(sQ, sQ.n)])
            em.flush()
        if stage == 2:
            em.barrier([sP, sV, sA], engines=("sync",))
            em.op("sync", lambda h: h.dma_start(out=taps["t_ofm"], in_=o_fm[:, :]), tick=sQ, tickk=16)
            em.op("sync", lambda h: h.dma_start(out=taps["t_qn"], in_=qn_g[:, :]), tick=sQ, tickk=16)
            em.op("sync", lambda h: h.dma_start(out=taps["t_qpe"], in_=qpe_g[:, :]), tick=sQ, tickk=16)
            em.op("sync", lambda h: h.dma_start(out=taps["t_kn"], in_=knope_g[:, :]), tick=sQ, tickk=16)
            em.op("sync", lambda h: h.dma_start(out=taps["t_v"], in_=v_g[:, :]), tick=sQ, tickk=16)
            em.op("sync", lambda h: h.dma_start(out=taps["t_srow"], in_=srow[:, :]), tick=sQ, tickk=16)
            em.op("sync", lambda h: h.dma_start(out=taps["t_abf"], in_=abf_r[:, :]), tick=sQ, tickk=16)
            em.wait("sync", [(sQ, sQ.n)])
            em.flush()


    es.close()
    return nc



def _rope_tables():
    inv = 1.0 / (10000.0 ** (np.arange(0, ROPE, 2, dtype=np.float64) / ROPE))
    pos = np.arange(S, dtype=np.float64)
    ang = pos[:, None] * inv[None, :]
    return np.cos(ang).T.astype(np.float32), np.sin(ang).T.astype(np.float32)


def _qmask(r):
    m = np.zeros((128, 256), dtype=np.float32)
    j = np.arange(128)
    if r == 0:
        m[:, 0:128][j[:, None] < j[None, :]] = NEG
        m[:, 128:256] = NEG
    else:
        m[:, 128:256][j[:, None] < j[None, :]] = NEG
    return m


def _bf(v):
    a = np.asarray(v)
    if a.dtype == np.uint16:
        a = a.view(BF16NP)
    return np.ascontiguousarray(a.astype(BF16NP) if a.dtype != BF16NP else a)


def pack_moe_weights(inputs):
    w1 = _bf(inputs["L0_ffn_w1"])
    w2 = _bf(inputs["L0_ffn_w2"])
    w3 = _bf(inputs["L0_ffn_w3"])
    sw1 = _bf(inputs["L0_ffn_sw1"])
    sw2 = _bf(inputs["L0_ffn_sw2"])
    sw3 = _bf(inputs["L0_ffn_sw3"])
    W13 = np.empty((18, 11, 128, 2, 16, 128), dtype=BF16NP)
    W13[0:2, :, :, 0] = sw1.reshape(2, 11, 128, 16, 128).transpose(0, 1, 4, 3, 2)
    W13[0:2, :, :, 1] = sw3.reshape(2, 11, 128, 16, 128).transpose(0, 1, 4, 3, 2)
    W13[2:, :, :, 0] = w1.reshape(16, 11, 128, 16, 128).transpose(0, 1, 4, 3, 2)
    W13[2:, :, :, 1] = w3.reshape(16, 11, 128, 16, 128).transpose(0, 1, 4, 3, 2)
    W2T = np.empty((18, 128, 11, 2048), dtype=BF16NP)
    W2T[0:2] = sw2.reshape(2048, 2, 11, 128).transpose(1, 3, 2, 0)
    W2T[2:] = w2.reshape(16, 2048, 11, 128).transpose(0, 3, 2, 1)
    return (np.ascontiguousarray(W13.reshape(18, 11, 128, 4096)),
            np.ascontiguousarray(W2T.reshape(18, 128, 11 * 2048)))


def stage_core_inputs(inputs, b, r):
    x = np.asarray(inputs["L0_attn_x"], dtype=np.float32)
    xb = np.ascontiguousarray(x[b])
    xm = np.ascontiguousarray(x[b].reshape(16, 128, D)[r::2].reshape(TPC, D))
    ck, sk = _rope_tables()
    idx = np.concatenate([np.arange((2 * i + r) * 128, (2 * i + r) * 128 + 128)
                          for i in range(NSLOT)])
    cq32 = ck[:, idx]
    sq32 = sk[:, idx]
    cq = np.ascontiguousarray(np.concatenate([cq32, cq32], axis=0).astype(np.float32))
    sq = np.ascontiguousarray(np.concatenate([sq32, sq32], axis=0).astype(np.float32))
    return {
        "xm": xm, "xb": xb,
        "wq": np.ascontiguousarray(np.asarray(inputs["L0_attn_wq"], np.float32)),
        "wkva": np.ascontiguousarray(np.asarray(inputs["L0_attn_wkv_a"], np.float32)),
        "wkvb": np.ascontiguousarray(np.asarray(inputs["L0_attn_wkv_b"], np.float32)),
        "wo": np.ascontiguousarray(np.asarray(inputs["L0_attn_wo"], np.float32)),
        "kvs": np.ascontiguousarray(np.asarray(inputs["L0_attn_kv_norm_scale"], np.float32)),
        "gw": np.ascontiguousarray(np.asarray(inputs["L0_ffn_gate_w"], np.float32)),
        "cq": cq, "sq": sq,
        "ck": np.ascontiguousarray(ck), "sk": np.ascontiguousarray(sk),
        "qmask": _qmask(r),
    }


def kernel(**inputs):
    nc = build_program(stage=4)
    W13, W2T = pack_moe_weights(inputs)
    in_maps = []
    for core in range(8):
        m = stage_core_inputs(inputs, core // 2, core % 2)
        m["w13"] = W13
        m["w2t"] = W2T
        in_maps.append(m)
    res = run_bass_kernel_spmd(nc, in_maps, core_ids=list(range(8)))
    out = np.zeros((B, S, D), dtype=np.float32)
    for core in range(8):
        b, r = core // 2, core % 2
        yc = res.results[core]["y"]
        for i in range(NSLOT):
            out[b, (2 * i + r) * 128:(2 * i + r) * 128 + 128, :] = \
                yc[:, i * 128:(i + 1) * 128].T
    return out


# version 10
# speedup vs baseline: 1.0855x; 1.0855x over previous
import numpy as np
import ml_dtypes
import concourse.bass as bass
import concourse.mybir as mybir
from concourse.bass_utils import run_bass_kernel_spmd

F32 = mybir.dt.float32
BF16 = mybir.dt.bfloat16
AF = mybir.ActivationFunctionType
ALU = mybir.AluOpType
AX = mybir.AxisListType
BF16NP = ml_dtypes.bfloat16

D = 2048
S = 2048
B = 4
H = 16
NOPE = 128
ROPE = 64
QH = NOPE + ROPE
LORA = 512
VD = 128
NSLOT = 8
TPC = NSLOT * 128
NEXP = 16
FI = 1408
FS = 2816
SCALE = float(QH) ** -0.5
EPS_LN = 1e-5
EPS_RMS = 1e-6
NEG = -3.0e38

STAGE = 1
FORCE_V_EVAC = False
DBG_DELAY = False
DBG_HM = None
W3_ON_SYNC = True
IT_BARRIER = False
IT_LIST = None
ENGS = ("tensor", "vector", "scalar", "gpsimd", "sync")


class Sem:
    def __init__(self, h):
        self.h = h
        self.n = 0

    def tick(self, k=1):
        self.n += k
        return self.n


class Emit:

    def __init__(self, block):
        self.block = block
        self.bufs = {e: [] for e in ENGS}
        self.lastw = {e: {} for e in ENGS}

    def _dedupe(self, eng, waits):
        m = {}
        for (s, v) in waits:
            if s is None:
                continue
            k = id(s)
            if k not in m or v > m[k][1]:
                m[k] = (s, int(v))
        lw = self.lastw[eng]
        wl = []
        for k, (s, v) in m.items():
            if lw.get(k, -1) >= v:
                continue
            lw[k] = v
            wl.append((s.h, v))
        return wl

    def op(self, eng, build, waits=(), tick=None, tickk=1):
        wl = self._dedupe(eng, waits)
        wp = tick.tick(tickk) if tick is not None else None
        th = tick.h if tick is not None else None

        def f(h, wl=wl, th=th, tickk=tickk, build=build):
            for (sh, v) in wl:
                h.wait_ge(sh, v)
            ins = build(h)
            if th is not None:
                ins.then_inc(th, tickk)

        self.bufs[eng].append(f)
        return wp

    def wait(self, eng, waits):
        wl = self._dedupe(eng, waits)

        def f(h, wl=wl):
            for (sh, v) in wl:
                h.wait_ge(sh, v)

        self.bufs[eng].append(f)

    def barrier(self, sems, engines=ENGS):
        vals = [(s, s.n) for s in sems]
        for e in engines:
            self.wait(e, vals)

    def flush(self):
        for eng in ENGS:
            fns = self.bufs[eng]
            if not fns:
                continue
            self.bufs[eng] = []
            dec = getattr(self.block, eng)

            def body(h, fns=fns):
                for f in fns:
                    f(h)

            dec(body)


def r3(t, k):
    return t.rearrange("p (k m) -> p k m", k=k)


class Arena:

    def __init__(self, nc, kb):
        self.base = nc.alloc_sbuf_tensor("arena", [128, kb * 1024], mybir.dt.uint8).ap()
        self.cursor = 0
        self.hi = 0

    def at(self, off, P, F, dtype):
        nb = F * mybir.dt.size(dtype)
        ap = self.base[:, off:off + nb].bitcast(dtype)
        if P < 128:
            ap = ap[0:P, :]
        self.hi = max(self.hi, off + nb)
        return ap

    def seq(self, P, F, dtype, align=64):
        off = (self.cursor + align - 1) // align * align
        ap = self.at(off, P, F, dtype)
        self.cursor = off + F * mybir.dt.size(dtype)
        return ap


def build_program(stage=STAGE, n_p1=NSLOT, ln_parts=4, do_kvs=True, do_w=True, n_groups=8, n_slots=NSLOT, n_iters=18, dbg_moe=False):
    nc = bass.Bass("TRN2", target_bir_lowering=False, debug=False)

    xm_d = nc.dram_tensor("xm", [TPC, D], F32, kind="ExternalInput").ap()
    xb_d = nc.dram_tensor("xb", [S, D], F32, kind="ExternalInput").ap()
    wq_d = nc.dram_tensor("wq", [D, H * QH], F32, kind="ExternalInput").ap()
    wkva_d = nc.dram_tensor("wkva", [D, LORA + ROPE], F32, kind="ExternalInput").ap()
    wkvb_d = nc.dram_tensor("wkvb", [LORA, H * (NOPE + VD)], F32, kind="ExternalInput").ap()
    wo_d = nc.dram_tensor("wo", [H * VD, D], F32, kind="ExternalInput").ap()
    kvs_d = nc.dram_tensor("kvs", [LORA], F32, kind="ExternalInput").ap()
    gw_d = nc.dram_tensor("gw", [NEXP, D], F32, kind="ExternalInput").ap()
    w13_d = nc.dram_tensor("w13", [18, 11, 128, 4096], BF16, kind="ExternalInput").ap()
    w2t_d = nc.dram_tensor("w2t", [18, 128, 11 * 2048], BF16, kind="ExternalInput").ap()
    cq_d = nc.dram_tensor("cq", [ROPE, TPC], F32, kind="ExternalInput").ap()
    sq_d = nc.dram_tensor("sq", [ROPE, TPC], F32, kind="ExternalInput").ap()
    ck_d = nc.dram_tensor("ck", [32, S], F32, kind="ExternalInput").ap()
    sk_d = nc.dram_tensor("sk", [32, S], F32, kind="ExternalInput").ap()
    qmask_d = nc.dram_tensor("qmask", [128, 256], F32, kind="ExternalInput").ap()

    y_d = None
    if stage >= 4:
        y_d = nc.dram_tensor("y", [D, TPC], F32, kind="ExternalOutput").ap()
    taps = {}
    if stage <= 1:
        taps["t_ln1m"] = nc.dram_tensor("t_ln1m", [128, 16 * TPC], BF16, kind="ExternalOutput").ap()
    if stage == 1:
        taps["t_ckv"] = nc.dram_tensor("t_ckv", [128, 4 * S], BF16, kind="ExternalOutput").ap()
        taps["t_kpe"] = nc.dram_tensor("t_kpe", [64, S], BF16, kind="ExternalOutput").ap()
    if stage == 2:
        taps["t_ofm"] = nc.dram_tensor("t_ofm", [128, 16 * TPC], BF16, kind="ExternalOutput").ap()
    if dbg_moe:
        taps["t_h"] = nc.dram_tensor("t_h", [128, 11 * 1024], BF16, kind="ExternalOutput").ap()
        taps["t_w13"] = nc.dram_tensor("t_w13", [128, 2 * 2 * 16 * 128], BF16, kind="ExternalOutput").ap()
        taps["t_w2a"] = nc.dram_tensor("t_w2a", [128, 6 * 2048], BF16, kind="ExternalOutput").ap()
        taps["t_ln2b"] = nc.dram_tensor("t_ln2b", [128, 16 * TPC], BF16, kind="ExternalOutput").ap()
    if stage == 3:
        taps["t_yacc"] = nc.dram_tensor("t_yacc", [128, 16 * TPC], F32, kind="ExternalOutput").ap()
        taps["t_ln2"] = nc.dram_tensor("t_ln2", [128, 16 * TPC], BF16, kind="ExternalOutput").ap()
        taps["t_dw"] = nc.dram_tensor("t_dw", [128, 128], F32, kind="ExternalOutput").ap()
        taps["t_dwT"] = nc.dram_tensor("t_dwT", [16, 1024], F32, kind="ExternalOutput").ap()
        taps["t_qn"] = nc.dram_tensor("t_qn", [128, 2 * 1024], BF16, kind="ExternalOutput").ap()
        taps["t_qpe"] = nc.dram_tensor("t_qpe", [64, 2 * 1024], BF16, kind="ExternalOutput").ap()
        taps["t_kn"] = nc.dram_tensor("t_kn", [128, 2 * 2048], BF16, kind="ExternalOutput").ap()
        taps["t_v"] = nc.dram_tensor("t_v", [128, 16 * 256], BF16, kind="ExternalOutput").ap()
        taps["t_srow"] = nc.dram_tensor("t_srow", [128, S], F32, kind="ExternalOutput").ap()
        taps["t_abf"] = nc.dram_tensor("t_abf", [128, S], BF16, kind="ExternalOutput").ap()

    id_bf_d = nc.inline_tensor(np.eye(128, dtype=BF16NP), "id_bf")
    id_f_d = nc.inline_tensor(np.eye(128, dtype=np.float32), "id_f")
    ones_col_d = nc.inline_tensor(np.ones((128, 1), dtype=BF16NP), "ones_col")
    ones_row_d = nc.inline_tensor(np.ones((1, 128), dtype=np.float32), "ones_row")
    oh_np = np.zeros((16, 16 * 128), dtype=BF16NP)
    for e in range(16):
        oh_np[e, e * 128:(e + 1) * 128] = 1
    oh_d = nc.inline_tensor(oh_np, "onehot")
    epsc_d = nc.inline_tensor(
        np.tile(np.array([[EPS_LN, EPS_RMS]], dtype=np.float32), (128, 1)), "epsc")

    from contextlib import ExitStack
    es = ExitStack()
    semn = ["sQ", "sG", "sC", "sP", "sV", "sA", "sO", "sPL", "sX0", "sX1",
            "sWQ", "sWO0", "sWO1", "sGW", "sW20", "sB0", "sB1", "sB2"]
    SH = {n: es.enter_context(nc.semaphore(n)) for n in semn}
    pb0 = es.enter_context(nc.psum_tensor("pb0", [128, 512], F32))
    pb1 = es.enter_context(nc.psum_tensor("pb1", [128, 512], F32))
    pb2 = es.enter_context(nc.psum_tensor("pb2", [128, 512], F32))
    pb3 = es.enter_context(nc.psum_tensor("pb3", [128, 512], F32))
    pb4 = es.enter_context(nc.psum_tensor("pb4", [128, 512], F32))
    pb5 = es.enter_context(nc.psum_tensor("pb5", [128, 512], F32))
    tpA = es.enter_context(nc.psum_tensor("tpA", [128, 1024], BF16))
    tpB = es.enter_context(nc.psum_tensor("tpB", [128, 1024], BF16))
    with nc.Block() as block:
        sQ, sG, sC, sP, sV, sA, sO = (Sem(SH[n]) for n in ("sQ", "sG", "sC", "sP", "sV", "sA", "sO"))
        sPL, sX0, sX1, sWQ, sWO0, sWO1, sGW, sW2 = (
            Sem(SH[n]) for n in ("sPL", "sX0", "sX1", "sWQ", "sWO0", "sWO1", "sGW", "sW20"))
        sW13 = [Sem(SH[n]) for n in ("sB0", "sB1", "sB2")]
        sX = [sX0, sX1]
        sWO = [sWO0, sWO1]
        ALL_SEMS = [sQ, sG, sC, sP, sV, sA, sPL, sX0, sX1, sWQ, sWO0, sWO1, sGW, sW2] + sW13
        em = Emit(block)
        pb = [pb0, pb1, pb2, pb3, pb4, pb5]
        tp = [tpA, tpB]

        ar = Arena(nc, 207)
        id_bf = ar.seq(128, 128, BF16)
        id_f = ar.seq(128, 128, F32)
        ones_col = ar.seq(128, 1, BF16)
        ones_row = ar.seq(1, 128, F32)
        kvsc = ar.seq(128, 4, F32)
        cols = ar.seq(128, 16, F32)
        st24 = ar.seq(128, 24, F32)
        ag2 = ar.seq(128, 2, F32)
        epsc = ar.seq(128, 2, F32)
        oh_sb = ar.seq(16, 16 * 128, BF16)
        att_base = ar.cursor
        ln1m = ar.seq(128, 16 * TPC, BF16)
        ckv = ar.seq(128, 4 * S, BF16)
        kpe = ar.seq(64, S, BF16)
        wkvb = ar.seq(128, 4 * 4096, BF16)
        p12_base = ar.cursor
        xt = [ar.seq(128, D, F32) for i in range(2)]
        lnbf = [ar.seq(128, D, BF16) for i in range(2)]
        lnst = ar.seq(128, 16 * 512, BF16)
        wkva = ar.seq(128, 16 * 576, BF16)
        ckvf = ar.seq(128, 4 * 512, F32)
        sqbf = ar.seq(128, 4 * 512, BF16)
        ckt = ar.seq(32, S, F32)
        skt = ar.seq(32, S, F32)
        kxw = ar.seq(32, 6 * 512, F32)
        rr_row = ar.seq(1, 2 * 512, F32)
        kvs4 = ar.seq(4, 128, F32)

        for dst, src in ((id_bf, id_bf_d), (id_f, id_f_d), (ones_col, ones_col_d),
                         (ones_row, ones_row_d), (epsc, epsc_d), (oh_sb, oh_d)):
            em.op("sync", lambda h, dst=dst, src=src: h.dma_start(out=dst[:, :], in_=src.ap()),
                  tick=sQ, tickk=16)
        em.op("sync", lambda h: h.dma_start(out=ckt[:, :], in_=ck_d), tick=sQ, tickk=16)
        em.op("sync", lambda h: h.dma_start(out=skt[:, :], in_=sk_d), tick=sQ, tickk=16)
        wp_consts = sQ.n

        wp_kvse = None
        if do_kvs:
            wp_kvsl = em.op("gpsimd", lambda h: h.dma_start(
                out=kvs4[:, :], in_=kvs_d.rearrange("(k p) -> k p", p=128)), tick=sPL, tickk=16)
            wp_kvst = em.op("tensor", lambda h: h.transpose(
                out=pb5[:, 0:4], in_=kvs4[:, :], identity=id_f[0:4, 0:4]),
                waits=[(sPL, 48), (sQ, wp_consts)], tick=sP)
            wp_kvse = em.op("vector", lambda h: h.tensor_copy(out=kvsc[:, :], in_=pb5[:, 0:4]),
                            waits=[(sP, wp_kvst)], tick=sV)

        wp_wkva = wp_wkvb = None
        if do_w:
            wp_wkva = em.op("gpsimd", lambda h: h.dma_start(
                out=r3(wkva, 16), in_=wkva_d.rearrange("(k p) o -> p k o", p=128)), tick=sPL, tickk=16)
            wp_wkvb = em.op("gpsimd", lambda h: h.dma_start(
                out=r3(wkvb, 4), in_=wkvb_d.rearrange("(k p) o -> p k o", p=128)), tick=sPL, tickk=16)
            wp_wkva = wp_wkvb = 48

        tpar = [0]
        tp_guard = [None, None]

        def ln_chunk(par, x_wait, dest, dest_k, dest_col, evac_extra_waits=()):
            def bns(h, par=par):
                h.bn_stats(out=st24[:, 0:6], in_=xt[par][:, 0:512])
                h.bn_stats(out=st24[:, 6:12], in_=xt[par][:, 512:1024])
                h.bn_stats(out=st24[:, 12:18], in_=xt[par][:, 1024:1536])
                return h.bn_stats(out=st24[:, 18:24], in_=xt[par][:, 1536:2048])
            wp1 = em.op("vector", bns, waits=[x_wait], tick=sV)
            wp2 = em.op("vector", lambda h: h.bn_aggr(out=ag2[:, :], in_=st24[:, :]),
                        waits=[(sV, wp1)], tick=sV)
            wp3 = em.op("scalar", lambda h: h.activation(
                out=cols[:, 0:1], in_=ag2[:, 1:2], func=AF.Sqrt, bias=epsc[:, 0:1]),
                waits=[(sV, wp2)], tick=sA)
            wp4 = em.op("vector", lambda h: h.reciprocal(out=cols[:, 1:2], in_=cols[:, 0:1]),
                        waits=[(sA, wp3)], tick=sV)
            wp5 = em.op("vector", lambda h: h.tensor_mul(out=cols[:, 2:3], in0=ag2[:, 0:1],
                                                         in1=cols[:, 1:2]),
                        waits=[(sV, wp4)], tick=sV)
            wp6 = em.op("vector", lambda h, par=par: h.tensor_scalar(
                out=lnbf[par][:, :], in0=xt[par][:, :],
                scalar1=cols[:, 1:2], scalar2=cols[:, 2:3],
                op0=ALU.mult, op1=ALU.subtract),
                waits=[(sV, wp5)], tick=sV)
            lastV = None
            lastA = None
            dv = r3(dest, dest_k)
            for g in range(2):
                half = tpar[0] % 2
                tpar[0] += 1
                twaits = [(sV, wp6), (sQ, wp_consts)]
                if tp_guard[half] is not None:
                    twaits.append(tp_guard[half])

                def tgrp(h, g=g, half=half, par=par):
                    ins = None
                    for j in range(8):
                        dc = 8 * g + j
                        ins = h.transpose(
                            out=tp[half][:, j * 128:(j + 1) * 128],
                            in_=lnbf[par][:, dc * 128:(dc + 1) * 128],
                            identity=id_bf[:, :])
                    return ins
                wpt = em.op("tensor", tgrp, waits=twaits, tick=sP)
                eng, sem = (("vector", sV) if (g % 2 == 0 or FORCE_V_EVAC) else ("scalar", sA))

                def evac(h, g=g, half=half, dv=dv, dest_col=dest_col, eng=eng):
                    src = r3(tp[half], 8)
                    dst = dv[:, 8 * g:8 * g + 8, dest_col * 128:(dest_col + 1) * 128]
                    if eng == "vector":
                        return h.tensor_copy(out=dst, in_=src)
                    return h.copy(out=dst, in_=src)
                ewaits = [(sP, wpt)] + list(evac_extra_waits)
                wpe = em.op(eng, evac, waits=ewaits, tick=sem)
                tp_guard[half] = (sem, wpe)
                if eng == "vector":
                    lastV = wpe
                else:
                    lastA = wpe
            done = [(sem_, wp_) for (sem_, wp_) in ((sV, lastV), (sA, lastA)) if wp_ is not None]
            return done, wp6

        xm_t = xm_d.rearrange("(c p) d -> c p d", p=128)
        xb_t = xb_d.rearrange("(c p) d -> c p d", p=128)
        xt_guard = [None, None]
        for cm in range(n_p1):
            par = cm % 2

            def ldx(h, cm=cm, par=par):
                return h.dma_start(out=xt[par][:, :], in_=xm_t[cm, :, :])
            wp_x = em.op("gpsimd", ldx, waits=(xt_guard[par] or []), tick=sX[par], tickk=16)
            done, _ = ln_chunk(par, (sX[par], wp_x), ln1m, 16, cm)
            xt_guard[par] = done

        wkva_v = r3(wkva, 16)
        ckv_v = r3(ckv, 4)
        SKIP_P2 = (stage == 0)
        kva_last = None
        psum_guard = []
        pb5_guard = None
        pb6_guard = []
        for tt in range(4 if not SKIP_P2 else 0):
            stg_done = []
            for cc in range(4):
                c = tt * 4 + cc
                par = c % 2

                def ldx(h, c=c, par=par):
                    return h.dma_start(out=xt[par][:, :], in_=xb_t[c, :, :])
                wp_x = em.op("gpsimd", ldx, waits=(xt_guard[par] or []), tick=sX[par], tickk=16)
                extra = [( sP, kva_last[1])] if kva_last is not None else []
                done, _ = ln_chunk(par, (sX[par], wp_x), lnst, 16, cc, evac_extra_waits=extra)
                xt_guard[par] = done
                stg_done += done

            kva_wps = []
            for oc in range(5):
                P = 128 if oc < 4 else 64

                def kmm(h, oc=oc, P=P):
                    ins = None
                    for dc in range(16):
                        ins = h.matmul(out=pb[oc][0:P, 0:512],
                                       lhsT=wkva_v[:, dc, oc * 128: oc * 128 + P],
                                       rhs=lnst[:, dc * 512:(dc + 1) * 512],
                                       start=(dc == 0), stop=(dc == 15))
                    return ins
                waits = list(stg_done) + [(sPL, 48)] + psum_guard
                kva_wps.append(em.op("tensor", kmm, waits=waits, tick=sP))
            kva_last = (sP, kva_wps[-1])

            ck_evacA = None
            for oc in range(4):
                def ev1(h, oc=oc):
                    return h.copy(out=ckvf[:, oc * 512:(oc + 1) * 512], in_=pb[oc][:, 0:512])
                em.op("scalar", ev1, waits=[(sP, kva_wps[oc])], tick=sA)

                def ev2(h, oc=oc):
                    return h.activation(out=sqbf[:, oc * 512:(oc + 1) * 512],
                                        in_=pb[oc][:, 0:512], func=AF.Square)
                ck_evacA = em.op("scalar", ev2, waits=[(sP, kva_wps[oc])], tick=sA)
            wp_kx1 = em.op("vector", lambda h: h.tensor_copy(out=kxw[:, 0:512], in_=pb4[0:32, 0:512]),
                           waits=[(sP, kva_wps[4])], tick=sV)
            wp_kx2 = em.op("vector", lambda h: h.tensor_copy(out=kxw[:, 512:1024], in_=pb4[32:64, 0:512]),
                           waits=[(sP, kva_wps[4])], tick=sV)
            psum_guard = [(sA, ck_evacA), (sV, wp_kx2)]

            def rms_mm(h):
                ins = None
                for oc in range(4):
                    ins = h.matmul(out=pb5[0:1, 0:512], lhsT=ones_col[:, :],
                                   rhs=sqbf[:, oc * 512:(oc + 1) * 512],
                                   start=(oc == 0), stop=(oc == 3))
                return ins
            rwaits = [(sA, ck_evacA), (sV, wp_kvse)] + pb6_guard + ([pb5_guard] if pb5_guard else [])
            wp_rms = em.op("tensor", rms_mm, waits=rwaits, tick=sP)
            wp_sq = em.op("scalar", lambda h: h.activation(
                out=rr_row[0:1, 0:512], in_=pb5[0:1, 0:512], func=AF.Sqrt,
                scale=1.0 / float(LORA), bias=epsc[0:1, 1:2]),
                waits=[(sP, wp_rms)], tick=sA)
            pb5_guard = (sA, wp_sq)
            wp_rc = em.op("vector", lambda h: h.reciprocal(out=rr_row[0:1, 512:1024],
                                                           in_=rr_row[0:1, 0:512]),
                          waits=[(sA, wp_sq)], tick=sV)
            wp_bc = em.op("tensor", lambda h: h.matmul(
                out=pb5[:, 0:512], lhsT=ones_row[:, :], rhs=rr_row[0:1, 512:1024],
                start=True, stop=True),
                waits=[(sV, wp_rc)], tick=sP)

            pb6_guard = []
            for oc in range(4):
                def nrm(h, oc=oc, tt=tt):
                    return h.scalar_tensor_tensor(
                        out=ckv_v[:, oc, tt * 512:(tt + 1) * 512],
                        in0=ckvf[:, oc * 512:(oc + 1) * 512],
                        scalar=kvsc[:, oc:oc + 1],
                        in1=pb5[:, 0:512],
                        op0=ALU.mult, op1=ALU.mult)
                wpn = em.op("vector", nrm, waits=[(sP, wp_bc), (sV, wp_kvse)], tick=sV)
                pb6_guard.append((sV, wpn))

            def kprod(h, tt=tt):
                h.tensor_mul(out=kxw[:, 1024:1536], in0=kxw[:, 0:512],
                             in1=ckt[:, tt * 512:(tt + 1) * 512])
                h.tensor_mul(out=kxw[:, 1536:2048], in0=kxw[:, 512:1024],
                             in1=skt[:, tt * 512:(tt + 1) * 512])
                h.tensor_mul(out=kxw[:, 2048:2560], in0=kxw[:, 0:512],
                             in1=skt[:, tt * 512:(tt + 1) * 512])
                return h.tensor_mul(out=kxw[:, 2560:3072], in0=kxw[:, 512:1024],
                                    in1=ckt[:, tt * 512:(tt + 1) * 512])
            wp_kp = em.op("vector", kprod, waits=[(sV, wp_kx2), (sQ, wp_consts)], tick=sV)

            def kfin(h, tt=tt):
                h.tensor_sub(out=kpe[0:32, tt * 512:(tt + 1) * 512],
                             in0=kxw[:, 1024:1536], in1=kxw[:, 1536:2048])
                return h.tensor_add(out=kpe[32:64, tt * 512:(tt + 1) * 512],
                                    in0=kxw[:, 2560:3072], in1=kxw[:, 2048:2560])
            em.op("vector", kfin, waits=[(sV, wp_kp)], tick=sV)


        if stage >= 2:
            em.barrier(ALL_SEMS)
            ar.cursor = p12_base
            wq_g = ar.seq(128, 16 * 384, BF16)
            knope_g = ar.seq(128, 2 * 2048, BF16)
            v_g = ar.seq(128, 16 * 256, BF16)
            qn_g = ar.seq(128, 2 * 1024, BF16)
            qpe_g = ar.seq(64, 2 * 1024, BF16)
            cq64 = ar.seq(64, TPC, F32)
            sq64 = ar.seq(64, TPC, F32)
            qrw = ar.seq(64, 2 * 1024, F32)
            qab = ar.seq(64, 2 * 1024, F32)
            srow = ar.seq(128, S, F32)
            abf_r = ar.seq(128, S, BF16)
            anm_r = abf_r
            atT = ar.seq(128, S, BF16)
            qmk = ar.seq(128, 256, F32)
            o_fm = ar.seq(128, 16 * TPC, BF16)

            wp_cq = em.op("sync", lambda h: h.dma_start(out=cq64[:, :], in_=cq_d), tick=sQ, tickk=16)
            wp_sq_ = em.op("sync", lambda h: h.dma_start(out=sq64[:, :], in_=sq_d), tick=sQ, tickk=16)
            wp_qm = em.op("sync", lambda h: h.dma_start(out=qmk[:, :], in_=qmask_d), tick=sQ, tickk=16)
            wp_tab = sQ.n

            wkvb_4 = wkvb.rearrange("p (k e c) -> p k e c", k=4, e=16)
            ln1m_v = r3(ln1m, 16)
            wqg_v = r3(wq_g, 16)

            bankg = {}
            evac_flip = [0]

            def ev_eng():
                evac_flip[0] ^= 1
                return ("vector", sV) if evac_flip[0] else ("scalar", sA)

            def accum(bank_t, bank_key, mms, evac, mm_waits, ev_waits, eng=None, sem=None):
                w = list(mm_waits)
                if bank_key in bankg:
                    w.append(bankg[bank_key])
                wpm = em.op("tensor", mms, waits=w, tick=sP)
                if eng is None:
                    eng, sem = ev_eng()
                wpe = em.op(eng, evac, waits=[(sP, wpm)] + list(ev_waits), tick=sem)
                bankg[bank_key] = (sem, wpe)
                return (sem, wpe), wpm

            wq_guard = None
            wp_q_mm_last = None
            grp_guard = []
            prev_exp = None
            prev_tr = None
            prev_av = None
            for g2 in range(n_groups):
                h0 = 2 * g2

                def ldwq(h, g2=g2):
                    return h.dma_start(
                        out=wqg_v,
                        in_=wq_d.rearrange("(k p) o -> p k o", p=128)[:, :, g2 * 384:(g2 + 1) * 384])
                wqw = [wq_guard] if wq_guard is not None else []
                wp_wq = em.op("gpsimd", ldwq, waits=wqw, tick=sWQ, tickk=16)

                kn_done = []
                for hh in range(2):
                    for kt in range(4):
                        bank = pb4 if (kt + hh) % 2 == 0 else pb5
                        bkey = "kv4" if (kt + hh) % 2 == 0 else "kv5"

                        def mms(h, hh=hh, kt=kt, bank=bank, h0=h0):
                            ins = None
                            for kc in range(4):
                                ins = h.matmul(
                                    out=bank[:, 0:512],
                                    lhsT=wkvb_4[:, kc, h0 + hh, 0:128],
                                    rhs=ckv_v[:, kc, kt * 512:(kt + 1) * 512],
                                    start=(kc == 0), stop=(kc == 3))
                            return ins

                        def evac(h, hh=hh, kt=kt, bank=bank, eng=None):
                            return None
                        eng, sem = ev_eng()

                        def evac2(h, hh=hh, kt=kt, bank=bank, eng=eng):
                            dst = knope_g[:, hh * 2048 + kt * 512: hh * 2048 + (kt + 1) * 512]
                            if eng == "vector":
                                return h.tensor_copy(out=dst, in_=bank[:, 0:512])
                            return h.copy(out=dst, in_=bank[:, 0:512])
                        d, _ = accum(bank, bkey, mms, evac2,
                                     [(sPL, 48)], grp_guard, eng=eng, sem=sem)
                        kn_done.append(d)

                v_done = []
                for tc in range(16):
                    bank = pb4 if tc % 2 == 0 else pb5
                    bkey = "kv4" if tc % 2 == 0 else "kv5"

                    def mms(h, tc=tc, bank=bank, h0=h0):
                        ins = None
                        for hh in range(2):
                            for kc in range(4):
                                ins = h.matmul(
                                    out=bank[:, hh * 128:(hh + 1) * 128],
                                    lhsT=ckv_v[:, kc, tc * 128:(tc + 1) * 128],
                                    rhs=wkvb_4[:, kc, h0 + hh, 128:256],
                                    start=(kc == 0), stop=(kc == 3))
                        return ins
                    eng, sem = ev_eng()

                    def evac2(h, tc=tc, bank=bank, eng=eng):
                        dst = v_g[:, tc * 256:(tc + 1) * 256]
                        if eng == "vector":
                            return h.tensor_copy(out=dst, in_=bank[:, 0:256])
                        return h.copy(out=dst, in_=bank[:, 0:256])
                    d, _ = accum(bank, bkey, mms, evac2, [(sPL, 48)], grp_guard, eng=eng, sem=sem)
                    v_done.append(d)

                q_done = []
                qpe_evs = []
                for hh in range(2):
                    for nb in range(2):
                        bank = pb4 if nb == 0 else pb5
                        bkey = "kv4" if nb == 0 else "kv5"

                        def mms(h, hh=hh, nb=nb, bank=bank):
                            ins = None
                            for dc in range(16):
                                ins = h.matmul(
                                    out=bank[:, 0:512],
                                    lhsT=wqg_v[:, dc, hh * 192: hh * 192 + 128],
                                    rhs=ln1m_v[:, dc, nb * 512:(nb + 1) * 512],
                                    start=(dc == 0), stop=(dc == 15))
                            return ins
                        eng, sem = ev_eng()

                        def evac2(h, hh=hh, nb=nb, bank=bank, eng=eng):
                            dst = qn_g[:, hh * 1024 + nb * 512: hh * 1024 + (nb + 1) * 512]
                            if eng == "vector":
                                return h.tensor_copy(out=dst, in_=bank[:, 0:512])
                            return h.copy(out=dst, in_=bank[:, 0:512])
                        d, wpm = accum(bank, bkey, mms, evac2, [(sWQ, wp_wq)], grp_guard, eng=eng, sem=sem)
                        q_done.append(d)
                        wp_q_mm_last = wpm

                        def mms_pe(h, hh=hh, nb=nb, bank=bank):
                            ins = None
                            for dc in range(16):
                                ins = h.matmul(
                                    out=bank[0:64, 0:512],
                                    lhsT=wqg_v[:, dc, hh * 192 + 128: hh * 192 + 192],
                                    rhs=ln1m_v[:, dc, nb * 512:(nb + 1) * 512],
                                    start=(dc == 0), stop=(dc == 15))
                            return ins

                        def evac_pe(h, hh=hh, nb=nb, bank=bank):
                            h.tensor_copy(out=qab[hh * 32:hh * 32 + 32, nb * 512:(nb + 1) * 512],
                                          in_=bank[0:32, 0:512])
                            return h.tensor_copy(out=qab[hh * 32:hh * 32 + 32, 1024 + nb * 512: 1024 + (nb + 1) * 512],
                                                 in_=bank[32:64, 0:512])
                        d, wpm = accum(bank, bkey, mms_pe, evac_pe, [(sWQ, wp_wq)], grp_guard,
                                       eng="vector", sem=sV)
                        qpe_evs.append(d)
                        wp_q_mm_last = wpm
                wq_guard = (sP, wp_q_mm_last)

                def qp1(h):
                    h.tensor_mul(out=qrw[:, 0:1024], in0=qab[:, 0:1024], in1=cq64[:, :])
                    return h.tensor_mul(out=qrw[:, 1024:2048], in0=qab[:, 1024:2048], in1=sq64[:, :])
                wp_a = em.op("vector", qp1, waits=qpe_evs + [(sQ, wp_tab)] + grp_guard, tick=sV)

                def qp2(h):
                    ins = None
                    for hh in range(2):
                        ins = h.tensor_sub(
                            out=qpe_g[0:32, hh * 1024:(hh + 1) * 1024],
                            in0=qrw[hh * 32:hh * 32 + 32, 0:1024],
                            in1=qrw[hh * 32:hh * 32 + 32, 1024:2048])
                    return ins
                wp_b = em.op("vector", qp2, waits=[(sV, wp_a)], tick=sV)

                def qp3(h):
                    h.tensor_mul(out=qrw[:, 0:1024], in0=qab[:, 0:1024], in1=sq64[:, :])
                    return h.tensor_mul(out=qrw[:, 1024:2048], in0=qab[:, 1024:2048], in1=cq64[:, :])
                wp_c = em.op("vector", qp3, waits=[(sV, wp_b)], tick=sV)

                def qp4(h):
                    ins = None
                    for hh in range(2):
                        ins = h.tensor_add(
                            out=qpe_g[32:64, hh * 1024:(hh + 1) * 1024],
                            in0=qrw[hh * 32:hh * 32 + 32, 1024:2048],
                            in1=qrw[hh * 32:hh * 32 + 32, 0:1024])
                    return ins
                wp_qf = em.op("vector", qp4, waits=[(sV, wp_c)], tick=sV)
                q_ready = [(sV, wp_qf)] + q_done

                tflip = [0]
                for hh in range(2):
                    hglob = h0 + hh
                    for i in range(n_slots):
                        klen = (2 * i + 2) * 128
                        nblk = (klen + 511) // 512
                        sc_ev = []
                        for kb in range(nblk):
                            w0 = kb * 512
                            wid = min(512, klen - w0)
                            bank = pb0 if kb % 2 == 0 else pb1
                            bkey = "s0" if kb % 2 == 0 else "s1"

                            def mms(h, hh=hh, i=i, w0=w0, wid=wid, bank=bank):
                                h.matmul(out=bank[:, 0:wid],
                                         lhsT=qn_g[:, hh * 1024 + i * 128:(hh * 1024 + (i + 1) * 128)],
                                         rhs=knope_g[:, hh * 2048 + w0: hh * 2048 + w0 + wid],
                                         start=True, stop=False)
                                return h.matmul(out=bank[:, 0:wid],
                                                lhsT=qpe_g[:, hh * 1024 + i * 128:(hh * 1024 + (i + 1) * 128)],
                                                rhs=kpe[:, w0:w0 + wid],
                                                start=False, stop=True)

                            last = (kb == nblk - 1)
                            eng, sem = ("vector", sV)

                            def evac2(h, w0=w0, wid=wid, bank=bank, last=last, klen=klen):
                                if not last:
                                    return h.tensor_copy(out=srow[:, w0:w0 + wid], in_=bank[:, 0:wid])
                                if wid > 256:
                                    h.tensor_copy(out=srow[:, w0:w0 + wid - 256], in_=bank[:, 0:wid - 256])
                                return h.tensor_tensor(out=srow[:, klen - 256:klen],
                                                       in0=bank[:, wid - 256:wid],
                                                       in1=qmk[:, :], op=ALU.add)
                            ev_extra = [prev_exp] if prev_exp is not None else []
                            d, _ = accum(bank, bkey, mms, evac2,
                                         q_ready + kn_done + [(sQ, wp_tab)], ev_extra,
                                         eng=eng, sem=sem)
                            sc_ev.append(d)

                        wp_mx = em.op("vector", lambda h, klen=klen: h.reduce_max(
                            out=cols[:, 4:5], in_=srow[:, 0:klen], axis=AX.X),
                            waits=sc_ev, tick=sV)
                        wp_ns = em.op("vector", lambda h: h.tensor_scalar(
                            out=cols[:, 5:6], in0=cols[:, 4:5], scalar1=-SCALE, scalar2=None,
                            op0=ALU.mult),
                            waits=[(sV, wp_mx)] + ([prev_exp] if prev_exp else []), tick=sV)
                        wp_ex = em.op("scalar", lambda h, klen=klen: h.activation(
                            out=abf_r[:, 0:klen], in_=srow[:, 0:klen], func=AF.Exp,
                            scale=SCALE, bias=cols[:, 5:6], accum_out=cols[:, 6:7]),
                            waits=[(sV, wp_ns)] + ([prev_tr] if prev_tr else []), tick=sA)
                        prev_exp = (sA, wp_ex)
                        wp_rc2 = em.op("vector", lambda h: h.reciprocal(
                            out=cols[:, 7:8], in_=cols[:, 6:7]),
                            waits=[(sA, wp_ex)], tick=sV)
                        wp_nm = em.op("vector", lambda h, klen=klen: h.tensor_scalar(
                            out=anm_r[:, 0:klen], in0=abf_r[:, 0:klen],
                            scalar1=cols[:, 7:8], scalar2=None, op0=ALU.mult),
                            waits=[(sV, wp_rc2)], tick=sV)

                        nt = klen // 128
                        at_ev = []
                        for t0 in range(0, nt, 8):
                            cnt = min(8, nt - t0)
                            half = tflip[0] % 2
                            tflip[0] += 1
                            bkey = "tpA" if half == 0 else "tpB"
                            w = [(sV, wp_nm)]
                            if bkey in bankg:
                                w.append(bankg[bkey])

                            def mms(h, t0=t0, cnt=cnt, half=half):
                                ins = None
                                for j in range(cnt):
                                    ins = h.transpose(
                                        out=tp[half][:, j * 128:(j + 1) * 128],
                                        in_=anm_r[:, (t0 + j) * 128:(t0 + j + 1) * 128],
                                        identity=id_bf[:, :])
                                return ins
                            wpm = em.op("tensor", mms, waits=w, tick=sP)
                            prev_tr = (sP, wpm)
                            eng, sem = ev_eng()

                            def evac2(h, t0=t0, cnt=cnt, half=half, eng=eng):
                                dst = atT[:, t0 * 128:(t0 + cnt) * 128]
                                src = tp[half][:, 0:cnt * 128]
                                if eng == "vector":
                                    return h.tensor_copy(out=dst, in_=src)
                                return h.copy(out=dst, in_=src)
                            wpe = em.op(eng, evac2,
                                        waits=[(sP, wpm)] + ([prev_av] if prev_av else []),
                                        tick=sem)
                            bankg[bkey] = (sem, wpe)
                            at_ev.append((sem, wpe))

                        bank = pb2 if (i + hh) % 2 == 0 else pb3
                        bkey = "o2" if (i + hh) % 2 == 0 else "o3"

                        def mms(h, hh=hh, nt=nt, bank=bank):
                            ins = None
                            for t in range(nt):
                                ins = h.matmul(
                                    out=bank[:, 0:128],
                                    lhsT=v_g[:, t * 256 + hh * 128: t * 256 + (hh + 1) * 128],
                                    rhs=atT[:, t * 128:(t + 1) * 128],
                                    start=(t == 0), stop=(t == nt - 1))
                            return ins
                        eng, sem = ev_eng()

                        def evac2(h, hglob=hglob, i=i, bank=bank, eng=eng):
                            dst = o_fm[:, hglob * 1024 + i * 128: hglob * 1024 + (i + 1) * 128]
                            if eng == "vector":
                                return h.tensor_copy(out=dst, in_=bank[:, 0:128])
                            return h.copy(out=dst, in_=bank[:, 0:128])
                        d, wpm_av = accum(bank, bkey, mms, evac2, at_ev + v_done, [],
                                          eng=eng, sem=sem)
                        prev_av = (sP, wpm_av)
                grp_guard = [prev_av]


        if stage >= 3:
            em.barrier(ALL_SEMS)
            align = lambda x: (x + 63) // 64 * 64
            yacc = ar.at(align(att_base), 128, 16 * TPC, F32)
            spare_att = align(att_base) + 16 * TPC * 4
            ov = align(p12_base)
            ln2 = ar.at(ov, 128, 16 * TPC, BF16)
            o_xt2 = align(ov + 32 * 1024)
            xt2 = [ar.at(o_xt2 + i * 8192, 128, D, F32) for i in range(2)]
            o_woc = o_xt2 + 16 * 1024
            wo_c = [ar.at(o_woc + i * 4096, 128, 16 * 128, BF16) for i in range(2)]
            o_sqx = o_woc + 8 * 1024
            sqx = ar.at(o_sqx, 128, 4 * 512, BF16)
            o_g = o_sqx + 4 * 1024
            gw_sb = ar.at(o_g, 16, D, BF16)
            gwT = ar.at(o_g + 4096, 128, 16 * 16, BF16)
            sg_sb = ar.at(o_g + 4608, 16, TPC, F32)
            sc_tm = ar.at(o_g + 8704, 128, 8 * 16, F32)
            m8t = ar.at(o_g + 9280, 128, 8, F32)
            dw_tm = ar.at(o_g + 9344, 128, 8 * 16, F32)
            row5 = ar.at(o_g + 9920, 1, 6 * 512, F32)
            dwT = ar.at(align(ov + 114 * 1024), 16, 8 * 128, BF16)
            assert o_g + 9920 + 6 * 512 * 4 <= p12_base + 83 * 1024

            yacc_v = r3(yacc, 16)
            ln2_v = r3(ln2, 16)
            ofm_v = r3(o_fm, 16)

            xg = [None, None]
            xev = []
            for cm in range(NSLOT):
                par = cm % 2

                def ldx(h, cm=cm, par=par):
                    return h.dma_start(out=xt2[par][:, :], in_=xm_t[cm, :, :])
                wp_x = em.op("gpsimd", ldx, waits=([xg[par]] if xg[par] else []), tick=sX[par], tickk=16)
                for g in range(4):
                    bank = pb0 if (cm * 4 + g) % 2 == 0 else pb1
                    bkey = "s0" if (cm * 4 + g) % 2 == 0 else "s1"
                    w = [(sX[par], wp_x), (sQ, wp_consts)]
                    if bkey in bankg:
                        w.append(bankg[bkey])

                    def mms(h, g=g, par=par, bank=bank):
                        ins = None
                        for j in range(4):
                            dc = 4 * g + j
                            ins = h.transpose(out=bank[:, j * 128:(j + 1) * 128],
                                              in_=xt2[par][:, dc * 128:(dc + 1) * 128],
                                              identity=id_f[:, :])
                        return ins
                    wpm = em.op("tensor", mms, waits=w, tick=sP)
                    eng, sem = ev_eng()

                    def evac2(h, g=g, cm=cm, bank=bank, eng=eng):
                        dst = yacc_v[:, 4 * g:4 * g + 4, cm * 128:(cm + 1) * 128]
                        src = bank.rearrange("p (j m) -> p j m", j=4)
                        if eng == "vector":
                            return h.tensor_copy(out=dst, in_=src)
                        return h.copy(out=dst, in_=src)
                    wpe = em.op(eng, evac2, waits=[(sP, wpm)], tick=sem)
                    bankg[bkey] = (sem, wpe)
                    xev.append((sem, wpe))
                    xg[par] = (sP, wpm)
            x_done = [xev[-2], xev[-1], xev[-3], xev[-4]]

            wog = [None, None]
            wo_r = wo_d.rearrange("(k p) c -> p k c", p=128)
            wo_done = []
            for oc in range(16):
                par = oc % 2

                def ldw(h, oc=oc, par=par):
                    return h.dma_start(out=r3(wo_c[par], 16),
                                      in_=wo_r[:, :, oc * 128:(oc + 1) * 128])
                wp_w = em.op("gpsimd", ldw, waits=([wog[par]] if wog[par] else []), tick=sWO[par], tickk=16)
                for nb in range(2):
                    bank = pb2 if (oc * 2 + nb) % 2 == 0 else pb3
                    bkey = "o2" if (oc * 2 + nb) % 2 == 0 else "o3"
                    w = [(sWO[par], wp_w)]
                    if bkey in bankg:
                        w.append(bankg[bkey])

                    def mms(h, oc=oc, nb=nb, par=par, bank=bank):
                        ins = None
                        for k in range(16):
                            ins = h.matmul(out=bank[:, 0:512],
                                           lhsT=wo_c[par][:, k * 128:(k + 1) * 128],
                                           rhs=ofm_v[:, k, nb * 512:(nb + 1) * 512],
                                           start=(k == 0), stop=(k == 15))
                        return ins
                    wpm = em.op("tensor", mms, waits=w, tick=sP)
                    wog[par] = (sP, wpm)

                    def evac2(h, oc=oc, nb=nb, bank=bank):
                        dst = yacc_v[:, oc, nb * 512:(nb + 1) * 512]
                        return h.scalar_tensor_tensor(out=dst, in0=bank[:, 0:512],
                                                      scalar=1.0, in1=dst,
                                                      op0=ALU.mult, op1=ALU.add)
                    wpe = em.op("vector", evac2, waits=[(sP, wpm)] + xev, tick=sV)
                    bankg[bkey] = (sV, wpe)
                    wo_done.append((sV, wpe))
            x2_done = wo_done[-4:]

            ln2_done = []
            g_sq = {}
            last_nrm = None
            for nb in range(2):
                wpx = None
                wpq = None
                for dc in range(16):
                    slot_b = (dc % 2)
                    slot_s = 2 + (dc % 2)
                    wb = [w for w in [g_sq.get(slot_b)] if w] + x2_done
                    if last_nrm is not None:
                        wb.append(last_nrm)
                    wp_cb = em.op("vector", lambda h, dc=dc, nb=nb, slot_b=slot_b: h.tensor_copy(
                        out=r3(sqx, 4)[:, slot_b, :],
                        in_=yacc_v[:, dc, nb * 512:(nb + 1) * 512]),
                        waits=wb, tick=sV)
                    ws = [w for w in [g_sq.get(slot_s)] if w] + x2_done
                    wp_sq2 = em.op("scalar", lambda h, dc=dc, nb=nb, slot_s=slot_s: h.activation(
                        out=r3(sqx, 4)[:, slot_s, :],
                        in_=yacc_v[:, dc, nb * 512:(nb + 1) * 512], func=AF.Square),
                        waits=ws, tick=sA)

                    def mmx(h, dc=dc, slot_b=slot_b):
                        return h.matmul(out=pb4[0:1, 0:512], lhsT=ones_col[:, :],
                                        rhs=r3(sqx, 4)[:, slot_b, :],
                                        start=(dc == 0), stop=(dc == 15))
                    wq_ = [(sV, wp_cb)]
                    if dc == 0 and "st4" in bankg:
                        wq_.append(bankg["st4"])
                    wpx = em.op("tensor", mmx, waits=wq_, tick=sP)

                    def mmq(h, dc=dc, slot_s=slot_s):
                        return h.matmul(out=pb5[0:1, 0:512], lhsT=ones_col[:, :],
                                        rhs=r3(sqx, 4)[:, slot_s, :],
                                        start=(dc == 0), stop=(dc == 15))
                    wq2_ = [(sA, wp_sq2)]
                    if dc == 0 and "st5" in bankg:
                        wq2_.append(bankg["st5"])
                    wpq = em.op("tensor", mmq, waits=wq2_, tick=sP)
                    g_sq[slot_b] = (sP, wpx)
                    g_sq[slot_s] = (sP, wpq)

                rv = r3(row5, 6)
                wp_m = em.op("vector", lambda h: h.tensor_scalar(
                    out=rv[0:1, 0, :], in0=pb4[0:1, 0:512], scalar1=1.0 / D, scalar2=None,
                    op0=ALU.mult), waits=[(sP, wpx)], tick=sV)
                wp_m2 = em.op("vector", lambda h: h.tensor_mul(
                    out=rv[0:1, 1, :], in0=rv[0:1, 0, :], in1=rv[0:1, 0, :]),
                    waits=[(sV, wp_m)], tick=sV)
                wp_var = em.op("vector", lambda h: h.scalar_tensor_tensor(
                    out=rv[0:1, 2, :], in0=pb5[0:1, 0:512], scalar=1.0 / D,
                    in1=rv[0:1, 1, :], op0=ALU.mult, op1=ALU.subtract),
                    waits=[(sV, wp_m2), (sP, wpq)], tick=sV)
                bankg["st4"] = (sV, wp_var)
                bankg["st5"] = (sV, wp_var)
                wp_sd = em.op("scalar", lambda h: h.activation(
                    out=rv[0:1, 3, :], in_=rv[0:1, 2, :], func=AF.Sqrt, bias=epsc[0:1, 0:1]),
                    waits=[(sV, wp_var)], tick=sA)
                wp_rs = em.op("vector", lambda h: h.reciprocal(
                    out=rv[0:1, 4, :], in_=rv[0:1, 3, :]), waits=[(sA, wp_sd)], tick=sV)
                wp_ms = em.op("vector", lambda h: h.tensor_mul(
                    out=rv[0:1, 5, :], in0=rv[0:1, 0, :], in1=rv[0:1, 4, :]),
                    waits=[(sV, wp_rs)], tick=sV)
                wrs = [(sV, wp_ms)]
                if "s0" in bankg:
                    wrs.append(bankg["s0"])
                wp_b1 = em.op("tensor", lambda h: h.matmul(
                    out=pb0[:, 0:512], lhsT=ones_row[:, :], rhs=rv[0:1, 4, :],
                    start=True, stop=True), waits=wrs, tick=sP)
                wms = [(sV, wp_ms)]
                if "s1" in bankg:
                    wms.append(bankg["s1"])
                wp_b2 = em.op("tensor", lambda h: h.matmul(
                    out=pb1[:, 0:512], lhsT=ones_row[:, :], rhs=rv[0:1, 5, :],
                    start=True, stop=True), waits=wms, tick=sP)
                nrm_done = []
                prev_nrm = None
                for dc in range(16):
                    def nrm1(h, dc=dc, nb=nb):
                        return h.tensor_tensor(out=r3(sqx, 4)[:, 0, :],
                                               in0=yacc_v[:, dc, nb * 512:(nb + 1) * 512],
                                               in1=pb0[:, 0:512], op=ALU.mult)
                    nw = [(sP, wp_b1), (sP, wp_b2)]
                    if prev_nrm is not None:
                        nw.append(prev_nrm)
                    wpn1 = em.op("vector", nrm1, waits=nw, tick=sV)

                    def nrm2(h, dc=dc, nb=nb):
                        return h.tensor_tensor(out=ln2_v[:, dc, nb * 512:(nb + 1) * 512],
                                               in0=r3(sqx, 4)[:, 0, :],
                                               in1=pb1[:, 0:512], op=ALU.subtract)
                    wpn = em.op("vector", nrm2, waits=[(sV, wpn1)], tick=sV)
                    prev_nrm = (sV, wpn)
                    nrm_done.append((sV, wpn))
                last_nrm = prev_nrm
                bankg["s0"] = nrm_done[-1]
                bankg["s1"] = nrm_done[-1]
                ln2_done += nrm_done

            wp_gw = em.op("gpsimd", lambda h: h.dma_start(out=gw_sb[:, :], in_=gw_d),
                          tick=sGW, tickk=16)
            gt_w = [(sGW, wp_gw), (sQ, wp_consts)]
            if "tpA" in bankg:
                gt_w.append(bankg["tpA"])

            def gw_tr(h):
                ins = None
                for dc in range(16):
                    ins = h.transpose(out=tpA[:, dc * 16:(dc + 1) * 16],
                                      in_=gw_sb[:, dc * 128:(dc + 1) * 128],
                                      identity=id_bf[0:16, 0:16])
                return ins
            wp_gt = em.op("tensor", gw_tr, waits=gt_w, tick=sP)
            wp_ge = em.op("vector", lambda h: h.tensor_copy(out=gwT[:, :], in_=tpA[:, 0:256]),
                          waits=[(sP, wp_gt)], tick=sV)
            bankg["tpA"] = (sV, wp_ge)

            sig_done = []
            for nb2 in range(2):
                bank = pb2 if nb2 == 0 else pb3
                bkey = "o2" if nb2 == 0 else "o3"

                def mms(h, nb2=nb2, bank=bank):
                    ins = None
                    for dc in range(16):
                        ins = h.matmul(out=bank[0:16, 0:512],
                                       lhsT=gwT[:, dc * 16:(dc + 1) * 16],
                                       rhs=ln2_v[:, dc, nb2 * 512:(nb2 + 1) * 512],
                                       start=(dc == 0), stop=(dc == 15))
                    return ins
                w = [(sV, wp_ge)] + ln2_done
                if bkey in bankg:
                    w.append(bankg[bkey])
                wpm = em.op("tensor", mms, waits=w, tick=sP)

                def sig(h, nb2=nb2, bank=bank):
                    return h.activation(out=sg_sb[:, nb2 * 512:(nb2 + 1) * 512],
                                        in_=bank[0:16, 0:512], func=AF.Sigmoid)
                wpe = em.op("scalar", sig, waits=[(sP, wpm)], tick=sA)
                bankg[bkey] = (sA, wpe)
                sig_done.append((sA, wpe))

            def sig_tr(h):
                ins = None
                for tc in range(8):
                    ins = h.transpose(out=pb3[:, tc * 16:(tc + 1) * 16],
                                      in_=sg_sb[:, tc * 128:(tc + 1) * 128],
                                      identity=id_f[0:16, 0:16])
                return ins
            w = sig_done + [bankg["o3"]]
            wp_st = em.op("tensor", sig_tr, waits=w, tick=sP)
            wp_se = em.op("vector", lambda h: h.tensor_copy(out=sc_tm[:, :], in_=pb3[:, 0:128]),
                          waits=[(sP, wp_st)], tick=sV)
            bankg["o3"] = (sV, wp_se)

            wp_dw = wp_se
            for tc in range(8):
                csl = lambda a, b: sc_tm[:, tc * 16 + a: tc * 16 + b]
                wp_m8 = em.op("vector", lambda h, tc=tc: h.max(
                    out=m8t[:, :], in_=sc_tm[:, tc * 16:(tc + 1) * 16]),
                    waits=[(sV, wp_dw)], tick=sV)
                wp_s = em.op("vector", lambda h: h.tensor_add(
                    out=cols[:, 8:9], in0=m8t[:, 0:1], in1=m8t[:, 1:2]),
                    waits=[(sV, wp_m8)], tick=sV)
                wp_r = em.op("vector", lambda h: h.reciprocal(
                    out=cols[:, 9:10], in_=cols[:, 8:9]), waits=[(sV, wp_s)], tick=sV)
                wp_w1 = em.op("vector", lambda h: h.tensor_mul(
                    out=cols[:, 10:11], in0=m8t[:, 0:1], in1=cols[:, 9:10]),
                    waits=[(sV, wp_r)], tick=sV)
                wp_w2 = em.op("vector", lambda h: h.tensor_mul(
                    out=cols[:, 11:12], in0=m8t[:, 1:2], in1=cols[:, 9:10]),
                    waits=[(sV, wp_w1)], tick=sV)
                wp_e1 = em.op("vector", lambda h, tc=tc: h.tensor_scalar(
                    out=dw_tm[:, tc * 16:(tc + 1) * 16], in0=sc_tm[:, tc * 16:(tc + 1) * 16],
                    scalar1=m8t[:, 0:1], scalar2=cols[:, 10:11],
                    op0=ALU.is_equal, op1=ALU.mult), waits=[(sV, wp_w2)], tick=sV)
                wp_e2 = em.op("vector", lambda h, tc=tc: h.tensor_scalar(
                    out=sc_tm[:, tc * 16:(tc + 1) * 16], in0=sc_tm[:, tc * 16:(tc + 1) * 16],
                    scalar1=m8t[:, 1:2], scalar2=cols[:, 11:12],
                    op0=ALU.is_equal, op1=ALU.mult), waits=[(sV, wp_e1)], tick=sV)
                wp_dw = em.op("vector", lambda h, tc=tc: h.tensor_add(
                    out=dw_tm[:, tc * 16:(tc + 1) * 16], in0=dw_tm[:, tc * 16:(tc + 1) * 16],
                    in1=sc_tm[:, tc * 16:(tc + 1) * 16]),
                    waits=[(sV, wp_e2)], tick=sV)

            def dw_tr(h):
                ins = None
                for tc in range(8):
                    bank = pb4 if tc < 4 else pb5
                    ins = h.transpose(out=bank[0:16, (tc % 4) * 128:((tc % 4) + 1) * 128],
                                      in_=dw_tm[:, tc * 16:(tc + 1) * 16],
                                      identity=id_f[:, :])
                return ins
            w = [(sV, wp_dw), bankg["st4"], bankg["st5"]]
            wp_dt = em.op("tensor", dw_tr, waits=w, tick=sP)

            def dw_ev(h):
                h.tensor_copy(out=dwT[:, 0:512], in_=pb4[0:16, 0:512])
                return h.tensor_copy(out=dwT[:, 512:1024], in_=pb5[0:16, 0:512])
            wp_de = em.op("vector", dw_ev, waits=[(sP, wp_dt)], tick=sV)
            bankg["st4"] = (sV, wp_de)
            bankg["st5"] = (sV, wp_de)
            dw_ready = (sV, wp_de)


        if stage >= 4:
            em.barrier(ALL_SEMS)
            ov2 = align(p12_base)
            h_sb = ar.at(ov2 + 32 * 1024, 128, 11 * 1024, BF16)
            w13 = ar.at(ov2 + 54 * 1024, 128, 3 * 2 * 16 * 128, BF16)
            w2a = ar.at(ov2 + 78 * 1024, 128, 6 * 2048, BF16)
            dwb = ar.at(ov2 + 102 * 1024, 128, TPC, BF16)
            ytmp = ar.at(ov2 + 104 * 1024, 128, 2 * 512, BF16)
            w2b = ar.at(spare_att, 128, 5 * 2048, BF16)
            sqx6 = ar.at(ov2 + 106 * 1024, 128, 4 * 512, BF16)
            assert ar.hi <= 207 * 1024, ar.hi

            w13_v = w13.rearrange("p (b w k m) -> p b w k m", b=3, w=2, k=16)
            h_v = r3(h_sb, 11)

            NFC = 11
            ranges = [(fc * 128, 128) for fc in range(NFC)]

            def w2strip(fc):
                return (w2a, fc) if fc < 6 else (w2b, fc - 6)

            w13_guard = {}
            sx_guard = {}
            w2_guard = None
            dwb_guard = None
            mm2_last = None
            h_guard = {}

            it_list = list(range(n_iters)) if IT_LIST is None else IT_LIST
            for it in it_list:
                if IT_BARRIER and it > 0:
                    em.barrier(ALL_SEMS)
                shared = it < 2
                if not shared:
                    e = it - 2

                def ldw2a(h, it=it):
                    return h.dma_start(out=w2a[:, :], in_=w2t_d[it, :, 0:6 * 2048])
                w = [w2_guard] if w2_guard is not None else []

                def ldw2b(h, it=it):
                    return h.dma_start(out=w2b[:, :], in_=w2t_d[it, :, 6 * 2048:11 * 2048])
                wp_w2all = None

                if not shared:
                    dwb_w = [dw_ready]
                    if dwb_guard is not None:
                        dwb_w.append(dwb_guard)
                    if "y4" in bankg:
                        dwb_w.append(bankg["y4"])
                    if "y5" in bankg:
                        dwb_w.append(bankg["y5"])

                    def bc1(h, e=e):
                        return h.matmul(out=pb4[:, 0:512], lhsT=oh_sb[:, e * 128:(e + 1) * 128],
                                        rhs=dwT[:, 0:512], start=True, stop=True)
                    wpb1 = em.op("tensor", bc1, waits=dwb_w, tick=sP)

                    def bc2(h, e=e):
                        return h.matmul(out=pb5[:, 0:512], lhsT=oh_sb[:, e * 128:(e + 1) * 128],
                                        rhs=dwT[:, 512:1024], start=True, stop=True)
                    wpb2 = em.op("tensor", bc2, waits=[], tick=sP)

                    def bce(h):
                        h.copy(out=dwb[:, 0:512], in_=pb4[:, 0:512])
                        return h.copy(out=dwb[:, 512:1024], in_=pb5[:, 0:512])
                    wpbe = em.op("scalar", bce, waits=[(sP, wpb1), (sP, wpb2)], tick=sA)
                    bankg["y4"] = (sA, wpbe)
                    bankg["y5"] = (sA, wpbe)
                    dwb_ready = (sA, wpbe)

                h_done = {}
                for ri, (r0, wd) in enumerate(ranges):
                    buf = (it * NFC + ri) % 3
                    wg = [w13_guard[buf]] if buf in w13_guard else []

                    def ld13(h, it=it, ri=ri, buf=buf):
                        return h.dma_start(out=w13[:, buf * 4096:(buf + 1) * 4096],
                                           in_=w13_d[it, ri])
                    wp3 = em.op("sync", ld13, waits=wg, tick=sW13[buf], tickk=16)
                    if ri == 2:
                        em.op("sync", ldw2a, waits=w, tick=sW2, tickk=16)
                        em.op("sync", ldw2b, waits=w, tick=sW2, tickk=16)
                        wp_w2all = sW2.n

                    nfb = 1
                    for fb in range(nfb):
                        fc = ri
                        for th in range(2):
                            ab = pb0 if th == 0 else pb1
                            abk = "a0" if th == 0 else "a1"
                            bb = pb2 if th == 0 else pb3
                            bbk = "b0" if th == 0 else "b1"

                            def mma(h, buf=buf, fb=fb, th=th, ab=ab):
                                ins = None
                                for dc in range(16):
                                    ins = h.matmul(
                                        out=ab[:, 0:512],
                                        lhsT=w13_v[:, buf, 0, dc, :],
                                        rhs=ln2_v[:, dc, th * 512:(th + 1) * 512],
                                        start=(dc == 0), stop=(dc == 15))
                                return ins
                            wa = [(sW13[buf], wp3)] + ln2_done
                            if abk in bankg:
                                wa.append(bankg[abk])
                            if fc in h_guard:
                                wa.append(h_guard[fc])
                            wpa = em.op("tensor", mma, waits=wa, tick=sP)

                            def mmb(h, buf=buf, fb=fb, th=th, bb=bb):
                                ins = None
                                for dc in range(16):
                                    ins = h.matmul(
                                        out=bb[:, 0:512],
                                        lhsT=w13_v[:, buf, 1, dc, :],
                                        rhs=ln2_v[:, dc, th * 512:(th + 1) * 512],
                                        start=(dc == 0), stop=(dc == 15))
                                return ins
                            wb = [(sW13[buf], wp3)]
                            if bbk in bankg:
                                wb.append(bankg[bbk])
                            wpb = em.op("tensor", mmb, waits=wb, tick=sP)
                            if ri == len(ranges) - 1 and fb == nfb - 1 and th == 1:
                                w13_guard[buf] = (sP, wpb)
                            else:
                                w13_guard[buf] = (sP, wpb)

                            def sil(h, th=th, ab=ab):
                                return h.activation(out=r3(sqx6, 4)[:, th, :],
                                                    in_=ab[:, 0:512], func=AF.Silu)
                            sw_ = [(sP, wpa)]
                            if th in sx_guard:
                                sw_.append(sx_guard[th])
                            wps = em.op("scalar", sil, waits=sw_, tick=sA)
                            bankg[abk] = (sA, wps)

                            def bcp(h, th=th, bb=bb):
                                return h.copy(out=r3(sqx6, 4)[:, 2 + th, :], in_=bb[:, 0:512])
                            bw_ = [(sP, wpb)]
                            if th in sx_guard:
                                bw_.append(sx_guard[th])
                            wpc = em.op("scalar", bcp, waits=bw_, tick=sA)
                            bankg[bbk] = (sA, wpc)

                            def hm(h, fc=fc, th=th):
                                if DBG_HM == "sil":
                                    return h.tensor_copy(
                                        out=h_v[:, fc, th * 512:(th + 1) * 512],
                                        in_=r3(sqx6, 4)[:, th, :])
                                if DBG_HM == "b":
                                    return h.tensor_copy(
                                        out=h_v[:, fc, th * 512:(th + 1) * 512],
                                        in_=r3(sqx6, 4)[:, 2 + th, :])
                                return h.tensor_mul(
                                    out=h_v[:, fc, th * 512:(th + 1) * 512],
                                    in0=r3(sqx6, 4)[:, th, :], in1=r3(sqx6, 4)[:, 2 + th, :])
                            hw = [(sA, wps), (sA, wpc)]
                            if mm2_last is not None:
                                hw.append(mm2_last)
                            wph = em.op("vector", hm, waits=hw, tick=sV)
                            h_done[(fc, th)] = (sV, wph)
                            sx_guard[th] = (sV, wph)

                all_h = list(h_done.values())
                for c in range(16):
                    for th in range(2):
                        yb = pb4 if (c * 2 + th) % 2 == 0 else pb5
                        ybk = "y4" if (c * 2 + th) % 2 == 0 else "y5"

                        def mm2(h, c=c, th=th, yb=yb):
                            ins = None
                            for fc in range(NFC):
                                t2, si = w2strip(fc)
                                ins = h.matmul(
                                    out=yb[:, 0:512],
                                    lhsT=t2[:, si * 2048 + c * 128: si * 2048 + (c + 1) * 128],
                                    rhs=h_v[:, fc, th * 512:(th + 1) * 512],
                                    start=(fc == 0), stop=(fc == NFC - 1))
                            return ins
                        wm = all_h + [(sW2, wp_w2all)]
                        if ybk in bankg:
                            wm.append(bankg[ybk])
                        wpm = em.op("tensor", mm2, waits=wm, tick=sP)
                        mm2_last = (sP, wpm)

                        if shared:
                            def yadd(h, c=c, th=th, yb=yb):
                                dst = yacc_v[:, c, th * 512:(th + 1) * 512]
                                return h.scalar_tensor_tensor(
                                    out=dst, in0=yb[:, 0:512], scalar=1.0, in1=dst,
                                    op0=ALU.mult, op1=ALU.add)
                            wpy = em.op("vector", yadd, waits=[(sP, wpm)], tick=sV)
                        else:
                            def ysc(h, th=th, yb=yb):
                                return h.tensor_tensor(
                                    out=r3(ytmp, 2)[:, th, :], in0=yb[:, 0:512],
                                    in1=dwb[:, th * 512:(th + 1) * 512], op=ALU.mult)
                            wpy0 = em.op("vector", ysc, waits=[(sP, wpm), dwb_ready], tick=sV)

                            def yad(h, c=c, th=th):
                                dst = yacc_v[:, c, th * 512:(th + 1) * 512]
                                return h.tensor_add(out=dst, in0=r3(ytmp, 2)[:, th, :], in1=dst)
                            wpy = em.op("vector", yad, waits=[(sV, wpy0)], tick=sV)
                            dwb_guard = (sV, wpy0)
                        bankg[ybk] = (sV, wpy)
                w2_guard = mm2_last
                for fc in range(NFC):
                    h_guard[fc] = mm2_last

            if dbg_moe:
                em.barrier([sP, sV, sA], engines=("sync",))
                em.op("sync", lambda h: h.dma_start(out=taps["t_h"], in_=h_sb[:, :]), tick=sQ, tickk=16)
                em.op("sync", lambda h: h.dma_start(out=taps["t_w13"], in_=w13[:, :]), tick=sQ, tickk=16)
                em.op("sync", lambda h: h.dma_start(out=taps["t_w2a"], in_=w2a[:, :]), tick=sQ, tickk=16)
                em.op("sync", lambda h: h.dma_start(out=taps["t_ln2b"], in_=ln2[:, :]), tick=sQ, tickk=16)
                em.wait("sync", [(sQ, sQ.n)])
            em.barrier([sP, sV, sA], engines=("gpsimd",))
            em.op("gpsimd", lambda h: h.dma_start(
                out=y_d.rearrange("(k p) t -> p k t", p=128), in_=yacc_v),
                tick=sO, tickk=16)
            em.wait("gpsimd", [(sO, sO.n)])
            em.wait("sync", [(sO, sO.n)])

        em.flush()

        if stage <= 1:
            em.barrier([sP, sV, sA], engines=("sync",))
            em.op("sync", lambda h: h.dma_start(out=taps["t_ln1m"], in_=ln1m[:, :]), tick=sQ, tickk=16)
            if stage == 1:
                em.op("sync", lambda h: h.dma_start(out=taps["t_ckv"], in_=ckv[:, :]), tick=sQ, tickk=16)
                em.op("sync", lambda h: h.dma_start(out=taps["t_kpe"], in_=kpe[:, :]), tick=sQ, tickk=16)
            em.wait("sync", [(sQ, sQ.n)])
            em.flush()
        if stage == 3:
            em.barrier([sP, sV, sA], engines=("sync",))
            em.op("sync", lambda h: h.dma_start(out=taps["t_yacc"], in_=yacc[:, :]), tick=sQ, tickk=16)
            em.op("sync", lambda h: h.dma_start(out=taps["t_ln2"], in_=ln2[:, :]), tick=sQ, tickk=16)
            em.op("sync", lambda h: h.dma_start(out=taps["t_dw"], in_=dw_tm[:, :]), tick=sQ, tickk=16)
            em.op("sync", lambda h: h.dma_start(out=taps["t_dwT"], in_=dwT[:, :]), tick=sQ, tickk=16)
            em.wait("sync", [(sQ, sQ.n)])
            em.flush()
        if stage == 2:
            em.barrier([sP, sV, sA], engines=("sync",))
            em.op("sync", lambda h: h.dma_start(out=taps["t_ofm"], in_=o_fm[:, :]), tick=sQ, tickk=16)
            em.op("sync", lambda h: h.dma_start(out=taps["t_qn"], in_=qn_g[:, :]), tick=sQ, tickk=16)
            em.op("sync", lambda h: h.dma_start(out=taps["t_qpe"], in_=qpe_g[:, :]), tick=sQ, tickk=16)
            em.op("sync", lambda h: h.dma_start(out=taps["t_kn"], in_=knope_g[:, :]), tick=sQ, tickk=16)
            em.op("sync", lambda h: h.dma_start(out=taps["t_v"], in_=v_g[:, :]), tick=sQ, tickk=16)
            em.op("sync", lambda h: h.dma_start(out=taps["t_srow"], in_=srow[:, :]), tick=sQ, tickk=16)
            em.op("sync", lambda h: h.dma_start(out=taps["t_abf"], in_=abf_r[:, :]), tick=sQ, tickk=16)
            em.wait("sync", [(sQ, sQ.n)])
            em.flush()


    es.close()
    return nc



def _rope_tables():
    inv = 1.0 / (10000.0 ** (np.arange(0, ROPE, 2, dtype=np.float64) / ROPE))
    pos = np.arange(S, dtype=np.float64)
    ang = pos[:, None] * inv[None, :]
    return np.cos(ang).T.astype(np.float32), np.sin(ang).T.astype(np.float32)


def _qmask(r):
    m = np.zeros((128, 256), dtype=np.float32)
    j = np.arange(128)
    if r == 0:
        m[:, 0:128][j[:, None] < j[None, :]] = NEG
        m[:, 128:256] = NEG
    else:
        m[:, 128:256][j[:, None] < j[None, :]] = NEG
    return m


def _bf(v):
    a = np.asarray(v)
    if a.dtype == np.uint16:
        a = a.view(BF16NP)
    return np.ascontiguousarray(a.astype(BF16NP) if a.dtype != BF16NP else a)


def pack_moe_weights(inputs):
    w1 = _bf(inputs["L0_ffn_w1"])
    w2 = _bf(inputs["L0_ffn_w2"])
    w3 = _bf(inputs["L0_ffn_w3"])
    sw1 = _bf(inputs["L0_ffn_sw1"])
    sw2 = _bf(inputs["L0_ffn_sw2"])
    sw3 = _bf(inputs["L0_ffn_sw3"])
    W13 = np.empty((18, 11, 128, 2, 16, 128), dtype=BF16NP)
    W13[0:2, :, :, 0] = sw1.reshape(2, 11, 128, 16, 128).transpose(0, 1, 4, 3, 2)
    W13[0:2, :, :, 1] = sw3.reshape(2, 11, 128, 16, 128).transpose(0, 1, 4, 3, 2)
    W13[2:, :, :, 0] = w1.reshape(16, 11, 128, 16, 128).transpose(0, 1, 4, 3, 2)
    W13[2:, :, :, 1] = w3.reshape(16, 11, 128, 16, 128).transpose(0, 1, 4, 3, 2)
    W2T = np.empty((18, 128, 11, 2048), dtype=BF16NP)
    W2T[0:2] = sw2.reshape(2048, 2, 11, 128).transpose(1, 3, 2, 0)
    W2T[2:] = w2.reshape(16, 2048, 11, 128).transpose(0, 3, 2, 1)
    return (np.ascontiguousarray(W13.reshape(18, 11, 128, 4096)),
            np.ascontiguousarray(W2T.reshape(18, 128, 11 * 2048)))


def stage_core_inputs(inputs, b, r):
    x = np.asarray(inputs["L0_attn_x"], dtype=np.float32)
    xb = np.ascontiguousarray(x[b])
    xm = np.ascontiguousarray(x[b].reshape(16, 128, D)[r::2].reshape(TPC, D))
    ck, sk = _rope_tables()
    idx = np.concatenate([np.arange((2 * i + r) * 128, (2 * i + r) * 128 + 128)
                          for i in range(NSLOT)])
    cq32 = ck[:, idx]
    sq32 = sk[:, idx]
    cq = np.ascontiguousarray(np.concatenate([cq32, cq32], axis=0).astype(np.float32))
    sq = np.ascontiguousarray(np.concatenate([sq32, sq32], axis=0).astype(np.float32))
    return {
        "xm": xm, "xb": xb,
        "wq": np.ascontiguousarray(np.asarray(inputs["L0_attn_wq"], np.float32)),
        "wkva": np.ascontiguousarray(np.asarray(inputs["L0_attn_wkv_a"], np.float32)),
        "wkvb": np.ascontiguousarray(np.asarray(inputs["L0_attn_wkv_b"], np.float32)),
        "wo": np.ascontiguousarray(np.asarray(inputs["L0_attn_wo"], np.float32)),
        "kvs": np.ascontiguousarray(np.asarray(inputs["L0_attn_kv_norm_scale"], np.float32)),
        "gw": np.ascontiguousarray(np.asarray(inputs["L0_ffn_gate_w"], np.float32)),
        "cq": cq, "sq": sq,
        "ck": np.ascontiguousarray(ck), "sk": np.ascontiguousarray(sk),
        "qmask": _qmask(r),
    }


def kernel(**inputs):
    nc = build_program(stage=4)
    W13, W2T = pack_moe_weights(inputs)
    in_maps = []
    for core in range(8):
        m = stage_core_inputs(inputs, core // 2, core % 2)
        m["w13"] = W13
        m["w2t"] = W2T
        in_maps.append(m)
    res = run_bass_kernel_spmd(nc, in_maps, core_ids=list(range(8)))
    out = np.zeros((B, S, D), dtype=np.float32)
    for core in range(8):
        b, r = core // 2, core % 2
        yc = res.results[core]["y"]
        for i in range(NSLOT):
            out[b, (2 * i + r) * 128:(2 * i + r) * 128 + 128, :] = \
                yc[:, i * 128:(i + 1) * 128].T
    return out
